# version 01
# baseline (speedup 1.0000x reference)
import contextlib
import numpy as np
import ml_dtypes
import concourse.bass as bass
import concourse.mybir as mybir
from concourse.bass_utils import run_bass_kernel_spmd

F32 = mybir.dt.float32
BF16 = mybir.dt.bfloat16
AF = mybir.ActivationFunctionType
ALU = mybir.AluOpType
AX = mybir.AxisListType

NCORES = 8
SEQ = 2048
C = 1024
H = 16
TKN = 128
ROWS = 4096
NT = ROWS // TKN
TPS = SEQ // TKN
CCENT = 63.5

_CFG = {"n_tiles": NT, "sublayers": ("F1", "A", "F2")}


def _build_nc():
    nc = bass.Bass("TRN2", target_bir_lowering=False)

    x_d = nc.dram_tensor("x", [ROWS, C], F32, kind="ExternalInput").ap()
    out_d = nc.dram_tensor("out", [ROWS, C], F32, kind="ExternalOutput").ap()
    WNAMES = ["fwrt", "fwkt", "fwvt", "wkt", "wvt", "wrt", "wot"]
    w_d = {n: nc.dram_tensor(n, [C, C], F32, kind="ExternalInput").ap()
           for n in WNAMES}
    tri_d = nc.dram_tensor("tri", [128, 128], F32, kind="ExternalInput").ap()
    idf_d = nc.dram_tensor("idf", [128, 128], F32, kind="ExternalInput").ap()
    ones1_d = nc.dram_tensor("ones1", [1, 128], F32, kind="ExternalInput").ap()
    cju_d = nc.dram_tensor("cju", [128, 1], F32, kind="ExternalInput").ap()
    cjw_d = nc.dram_tensor("cjw", [128, 1], F32, kind="ExternalInput").ap()
    tdr_d = nc.dram_tensor("tdr", [128, H], F32, kind="ExternalInput").ap()
    tfr_d = nc.dram_tensor("tfr", [128, H], F32, kind="ExternalInput").ap()
    mu_d = {n: nc.dram_tensor(n, [128, 8], F32, kind="ExternalInput").ap()
            for n in ["murt", "mukt", "mxkt", "mxvt", "mxrt"]}

    es = contextlib.ExitStack()
    sb = lambda name, shape, dt: es.enter_context(nc.sbuf_tensor(name, shape, dt))
    ps = lambda name, shape, dt: es.enter_context(nc.psum_tensor(name, shape, dt))
    sem = lambda name: es.enter_context(nc.semaphore(name))

    w_sb = {n: sb(n + "_s", [128, 8 * C], BF16) for n in WNAMES}
    Rb = [sb("R0", [128, C], F32), sb("R1", [128, C], F32)]
    z_tm = sb("z_tm", [128, C], BF16)
    zf = {s: sb("zf" + s, [128, 8 * 129], BF16) for s in ("F1", "A", "F2")}
    dbuf = sb("dbuf", [128, 8 * 128], BF16)
    xr_b = sb("xr", [128, 8 * 128], BF16)
    xk_b = sb("xk", [128, 8 * 128], BF16)
    xv_b = sb("xv", [128, 8 * 128], BF16)
    junk = sb("junk", [128, C], BF16)
    s1 = sb("s1", [128, 1], F32); s2 = sb("s2", [128, 1], F32)
    meanb = sb("meanb", [128, 1], F32); varb = sb("varb", [128, 1], F32)
    m2b = sb("m2b", [128, 1], F32); lnvb = sb("lnvb", [128, 1], F32)
    invb = sb("invb", [128, 1], F32); negmib = sb("negmib", [128, 1], F32)
    krelu = sb("krelu", [128, C], BF16)
    ka_tm = sb("ka_tm", [128, C], BF16)
    ka_fm = sb("ka_fm", [128, 8 * 128], BF16)
    esig = sb("esig", [128, C], F32)
    denb = sb("denb", [128, C], F32)
    recb = sb("recb", [128, C], F32)
    u_sb = sb("u_sb", [128, C], F32)
    ek = sb("ek", [128, C], F32)
    EKV = sb("EKV", [128, 1040], F32)
    ahat = sb("ahat", [128, 1040], F32)
    t2b = sb("t2b", [128, 1040], F32)
    tmpb = sb("tmpb", [128, 1040], F32)
    numb = sb("numb", [128, 1040], F32)
    qrow = sb("qrow", [1, 1040], F32)
    Aprime = sb("Aprime", [1, 1040], F32)
    gtmp = sb("gtmp", [128, C], F32)
    g2 = sb("g2", [128, C], BF16)
    recip16 = sb("recip16", [128, H], F32)
    gfm = sb("gfm", [128, 8 * 128], BF16)
    tri = sb("tri_s", [128, 128], F32)
    idf = sb("idf_s", [128, 128], F32)
    idb = sb("idb_s", [128, 128], BF16)
    ones1 = sb("ones1_s", [1, 128], F32)
    cju = sb("cju_s", [128, 1], F32)
    cjw = sb("cjw_s", [128, 1], F32)
    tdr = sb("tdr_s", [128, H], F32)
    tfr = sb("tfr_s", [128, H], F32)
    wtil = sb("wtil", [128, H], F32)
    negw = sb("negw", [128, H], F32)
    utab = sb("utab", [128, H], F32)
    wtab = sb("wtab", [128, H], F32)
    ew128 = sb("ew128", [128, H], F32)
    eu16 = sb("eu16", [128, H], F32)
    mu_sb = {n: sb(n + "_s", [128, 8], F32) for n in mu_d}

    p0 = ps("p0", [128, 1024], BF16)
    pg = [ps(f"p{k}", [128, 512], F32) for k in range(1, 7)]
    p1, p2, p3, p4, p5, p6 = pg

    semT = sem("semT"); semV = sem("semV"); semS = sem("semS")
    semDX = sem("semDX"); semDO = sem("semDO")
    semDW = sem("semDW"); semDC = sem("semDC")

    ops = {k: [] for k in ("t", "v", "s", "y", "g")}
    cnt = {"t": 0, "v": 0, "s": 0}
    SEMO = {"t": semT, "v": semV, "s": semS}

    def emit(stream, fn, waits=(), inc=None):
        ws = [(s_, v_) for (s_, v_) in waits if v_ > 0]
        ops[stream].append((ws, fn, inc))
        if inc is not None:
            cnt[inc] += 1

    def wT(): return (semT, cnt["t"])
    def wV(): return (semV, cnt["v"])
    def wS(): return (semS, cnt["s"])

    def ev(stream):
        return (SEMO[stream], cnt[stream])

    w3 = {n: w_sb[n][:].rearrange("p (cc d) -> p cc d", cc=8) for n in WNAMES}
    zf3 = {s: zf[s][:].rearrange("p (cc t) -> p cc t", cc=8) for s in zf}
    EKVv = EKV[:].rearrange("p (h c) -> p h c", h=H)
    ahv = ahat[:].rearrange("p (h c) -> p h c", h=H)
    t2v = t2b[:].rearrange("p (h c) -> p h c", h=H)
    tmpv = tmpb[:].rearrange("p (h c) -> p h c", h=H)
    numv = numb[:].rearrange("p (h c) -> p h c", h=H)
    qrv = qrow[:].rearrange("p (h c) -> p h c", h=H)
    Apv = Aprime[:].rearrange("p (h c) -> p h c", h=H)
    ekv16 = ek[:].rearrange("p (h d) -> p h d", h=H)
    gtv = gtmp[:].rearrange("p (h d) -> p h d", h=H)

    def bc(ap2d, n65, parts=128):
        return ap2d.unsqueeze(2).broadcast_to([parts, ap2d.shape[1], n65])

    def pro_y(sync):
        for nm, dst, src in [("tri", tri, tri_d), ("idf", idf, idf_d),
                             ("ones1", ones1, ones1_d), ("cju", cju, cju_d),
                             ("cjw", cjw, cjw_d), ("tdr", tdr, tdr_d),
                             ("tfr", tfr, tfr_d)]:
            sync.dma_start(dst[:], src).then_inc(semDC, 16)
        for nm in mu_d:
            sync.dma_start(mu_sb[nm][:], mu_d[nm]).then_inc(semDC, 16)
    ops["y"].append(([], pro_y, None))
    NCONST = 12

    def pro_g(gpsimd):
        for nm in WNAMES:
            src = w_d[nm].rearrange("(cc p) d -> p cc d", p=128)
            gpsimd.dma_start(w3[nm], src).then_inc(semDW, 16)
    ops["g"].append(([], pro_g, None))
    WIDX = {n: 16 * (k + 1) for k, n in enumerate(WNAMES)}

    emit("s", lambda s_: s_.activation(wtil[:], tdr[:], AF.Exp),
         waits=[(semDC, 16 * NCONST)], inc="s")
    emit("v", lambda v: v.tensor_scalar(negw[:], wtil[:], -1.0, None, ALU.mult),
         waits=[wS()], inc="v")

    def tabs(s_):
        s_.activation(utab[:], negw[:], AF.Exp, scale=cju[:])
        s_.activation(wtab[:], negw[:], AF.Exp, scale=cjw[:])
        s_.activation(ew128[:], negw[:], AF.Exp, scale=128.0)
        return s_.activation(eu16[:], tfr[:], AF.Exp)
    emit("s", tabs, waits=[wV()], inc="s")

    def pro_v(v):
        v.tensor_copy(idb[:], idf[:])
        v.tensor_scalar(w_sb["fwvt"][:], w_sb["fwvt"][:], 0.5, None, ALU.mult)
        v.memset(Aprime[:], 0.0)
        for s in zf3:
            v.memset(zf3[s][:, :, 0:1], 0.0)
        return v.nop()
    emit("v", pro_v, waits=[(semDW, WIDX["fwvt"]), (semDC, 16 * NCONST)],
         inc="v")
    EV_PRO_V = ev("v")

    n_tiles = _CFG["n_tiles"]
    for i in range(n_tiles):
        def load(sync, i=i):
            return sync.dma_start(Rb[i % 2][:], x_d[i * TKN:(i + 1) * TKN, :])
        waits = [(semDO, 16 * (i - 1))] if i >= 2 else []
        ops["y"].append((waits, lambda y, i=i: load(y, i).then_inc(semDX, 16),
                         None))

    last_rd = {id(p0): None}
    for p in pg:
        last_rd[id(p)] = None

    def rd_wait(*banks):
        ws = []
        for b in banks:
            e = last_rd[id(b)]
            if e is not None:
                ws.append(e)
        return ws

    def mark_rd(bank, event):
        last_rd[id(bank)] = event

    def layer_norm(R, i, first):
        w0 = [(semDX, 16 * (i + 1))] if first else []
        def v1(v):
            v.reduce_sum(s1[:], R[:], axis=AX.X)
            v.drain()
            return v.tensor_scalar(meanb[:], s1[:], 1.0 / C, None, ALU.mult)
        emit("v", v1, waits=w0 + [wS(), wT()], inc="v")
        emit("s", lambda s_: s_.activation(junk[:], R[:], AF.Square,
                                           accum_out=s2[:]),
             waits=w0 + [wV()], inc="s")
        def v2(v):
            v.drain()
            v.tensor_scalar(varb[:], s2[:], 1.0 / C, None, ALU.mult)
            v.tensor_scalar(m2b[:], meanb[:], meanb[:], None, ALU.mult)
            v.drain()
            v.tensor_sub(varb[:], varb[:], m2b[:])
            v.drain()
            return v.tensor_scalar(varb[:], varb[:], 1e-5, None, ALU.add)
        emit("v", v2, waits=[wS()], inc="v")
        def s2f(s_):
            s_.activation(lnvb[:], varb[:], AF.Ln)
            s_.drain()
            return s_.activation(invb[:], lnvb[:], AF.Exp, scale=-0.5)
        emit("s", s2f, waits=[wV()], inc="s")
        def v3(v):
            v.tensor_scalar(negmib[:], invb[:], meanb[:], -1.0,
                            ALU.mult, ALU.mult)
            v.drain()
            return v.tensor_scalar(z_tm[:], R[:], invb[:], negmib[:],
                                   ALU.mult, ALU.add)
        emit("v", v3, waits=[wS()], inc="v")
        return ev("v")

    def transpose_to(p0bank, src, ev_src):
        def tf(t):
            last = None
            for cc in range(8):
                last = t.transpose(p0bank[:, cc * 128:(cc + 1) * 128],
                                   src[:, cc * 128:(cc + 1) * 128], idb[:])
            return last
        emit("t", tf, waits=[ev_src, EV_PRO_V] + rd_wait(p0bank), inc="t")
        return ev("t")

    def evac_z(slayer, ev_t):
        emit("s", lambda s_: s_.copy(
            zf3[slayer][:, :, 1:129],
            p0[:].rearrange("p (cc t) -> p cc t", cc=8)),
            waits=[ev_t], inc="s")
        e = ev("s")
        mark_rd(p0, e)
        return e

    def mixes(slayer, i, targets, ev_evac):
        z3 = zf3[slayer]
        n = i % TPS
        def vm(v):
            if n == 0 and i > 0:
                v.memset(z3[:, :, 0:1], 0.0)
            v.tensor_sub(dbuf[:].rearrange("p (cc t) -> p cc t", cc=8),
                         z3[:, :, 1:129], z3[:, :, 0:128])
            v.drain()
            for (dst, mu) in targets:
                for cc in range(8):
                    v.scalar_tensor_tensor(
                        dst[:, cc * 128:(cc + 1) * 128],
                        dbuf[:, cc * 128:(cc + 1) * 128],
                        mu_sb[mu][:, cc:cc + 1],
                        zf[slayer][:, cc * 129:cc * 129 + 128],
                        ALU.mult, ALU.add)
            v.drain()
            return v.tensor_copy(z3[:, :, 0:1], z3[:, :, 128:129])
        emit("v", vm, waits=[ev_evac, wT()], inc="v")
        return ev("v")

    def proj(lhsbuf, wname, pa, pb, ev_mix, extra_w=()):
        wv = w3[wname]
        def tp(t):
            last = None
            for cc in range(8):
                st = cc == 0
                sp = cc == 7
                t.matmul(pa[:], lhsbuf[:, cc * 128:(cc + 1) * 128],
                         wv[:, cc, 0:512], start=st, stop=sp)
                last = t.matmul(pb[:], lhsbuf[:, cc * 128:(cc + 1) * 128],
                                wv[:, cc, 512:1024], start=st, stop=sp)
            return last
        waits = [ev_mix, (semDW, WIDX[wname]), EV_PRO_V] + rd_wait(pa, pb) \
            + list(extra_w)
        emit("t", tp, waits=waits, inc="t")
        return ev("t")

    def ffn(i, R, slayer, first):
        evz = layer_norm(R, i, first=first)
        evt = transpose_to(p0, z_tm[:], evz)
        evev = evac_z(slayer, evt)
        evm = mixes(slayer, i, [(xr_b, "murt"), (xk_b, "mukt")], evev)
        evr = proj(xr_b, "fwrt", p1, p2, evm)
        evk = proj(xk_b, "fwkt", p3, p4, evm)
        def sact(s_):
            s_.activation(krelu[:, 0:512], p3[:], AF.Relu)
            s_.activation(krelu[:, 512:1024], p4[:], AF.Relu)
            s_.activation(esig[:, 0:512], p1[:], AF.Exp, scale=-1.0)
            return s_.activation(esig[:, 512:1024], p2[:], AF.Exp, scale=-1.0)
        emit("s", sact, waits=[evk], inc="s")
        e = ev("s")
        mark_rd(p1, e); mark_rd(p2, e); mark_rd(p3, e); mark_rd(p4, e)
        def vka(v):
            v.tensor_mul(ka_tm[:], krelu[:], krelu[:])
            v.tensor_scalar(denb[:], esig[:], 1.0, None, ALU.add)
            v.drain()
            v.reciprocal(recb[:], denb[:])
            return v.drain()
        emit("v", vka, waits=[wS()], inc="v")
        evka = ev("v")
        evt2 = transpose_to(p0, ka_tm[:], evka)
        emit("s", lambda s_: s_.copy(ka_fm[:], p0[:]), waits=[evt2], inc="s")
        eevac = ev("s")
        mark_rd(p0, eevac)
        evq = proj(ka_fm, "fwvt", p3, p4, eevac)
        def vu(v):
            v.tensor_mul(u_sb[:, 0:512], recb[:, 0:512], p3[:])
            v.tensor_mul(u_sb[:, 512:1024], recb[:, 512:1024], p4[:])
            v.drain()
            return v.tensor_add(R[:], R[:], u_sb[:])
        wsto = [(semDO, 16 * (i - 1))] if i >= 2 else []
        emit("v", vu, waits=[evq] + wsto, inc="v")
        e = ev("v")
        mark_rd(p3, e); mark_rd(p4, e)
        return e

    def attn(i, R, first):
        n = i % TPS
        evz = layer_norm(R, i, first=first)
        evt = transpose_to(p0, z_tm[:], evz)
        evev = evac_z("A", evt)
        evm = mixes("A", i, [(xk_b, "mxkt"), (xv_b, "mxvt"), (xr_b, "mxrt")],
                    evev)
        evk = proj(xk_b, "wkt", p1, p2, evm)
        evv = proj(xv_b, "wvt", p3, p4, evm)
        evr = proj(xr_b, "wrt", p5, p6, evm)
        def sact(s_):
            s_.activation(ek[:, 0:512], p1[:], AF.Exp)
            s_.activation(ek[:, 512:1024], p2[:], AF.Exp)
            s_.activation(esig[:, 0:512], p5[:], AF.Exp, scale=-1.0)
            return s_.activation(esig[:, 512:1024], p6[:], AF.Exp, scale=-1.0)
        emit("s", sact, waits=[evr], inc="s")
        es_ = ev("s")
        mark_rd(p1, es_); mark_rd(p2, es_); mark_rd(p5, es_); mark_rd(p6, es_)
        def vek(v):
            if n == 0 and i > 0:
                v.memset(Aprime[:], 0.0)
            v.tensor_mul(EKVv[:, 0:8, 0:64],
                         ek[:, 0:512].rearrange("p (h d) -> p h d", h=8),
                         p3[:].rearrange("p (h d) -> p h d", h=8))
            v.tensor_mul(EKVv[:, 8:16, 0:64],
                         ek[:, 512:1024].rearrange("p (h d) -> p h d", h=8),
                         p4[:].rearrange("p (h d) -> p h d", h=8))
            v.reduce_sum(EKVv[:, :, 64], ekv16, axis=AX.X)
            v.drain()
            v.tensor_mul(ahv, EKVv, bc(utab[:], 65))
            v.tensor_scalar(denb[:], esig[:], 1.0, None, ALU.add)
            v.drain()
            v.reciprocal(recb[:], denb[:])
            return v.drain()
        emit("v", vek, waits=[wS()], inc="v")
        evek = ev("v")
        mark_rd(p3, evek); mark_rd(p4, evek)
        PG = [p1, p2, p3, p4]
        def twkv(t):
            last = None
            for g in range(4):
                t.matmul(PG[g][:, 0:260], tri[:],
                         ahat[:, 260 * g:260 * (g + 1)],
                         start=True, stop=False)
                last = t.matmul(PG[g][:, 0:260], ones1[:],
                                Aprime[0:1, 260 * g:260 * (g + 1)],
                                start=False, stop=True)
            return last
        emit("t", twkv, waits=[evek] + rd_wait(p1, p2, p3, p4), inc="t")
        evw = ev("t")
        def vepi(v):
            for g in range(4):
                v.tensor_mul(t2v[:, 4 * g:4 * (g + 1), :],
                             PG[g][:, 0:260].rearrange("p (h c) -> p h c", h=4),
                             bc(wtab[:, 4 * g:4 * (g + 1)], 65))
            v.tensor_mul(tmpv, EKVv, bc(eu16[:], 65))
            v.tensor_mul(t2v[0:1, :, :], Apv,
                         bc(wtab[0:1, :], 65, parts=1))
            v.drain()
            v.tensor_scalar(tmpv[:, :, 64], tmpv[:, :, 64], 1e-6, None,
                            ALU.max)
            v.drain()
            v.tensor_add(numb[:], t2b[:], tmpb[:])
            v.drain()
            v.reciprocal(recip16[:], numv[:, :, 64])
            for g in range(4):
                v.tensor_mul(
                    Apv[0:1, 4 * g:4 * (g + 1), :],
                    PG[g][0:1, 0:260].rearrange("p (h c) -> p h c", h=4),
                    bc(ew128[0:1, 4 * g:4 * (g + 1)], 65, parts=1))
            v.drain()
            v.tensor_mul(gtv, numv[:, :, 0:64], bc(recip16[:], 64))
            v.drain()
            v.tensor_mul(g2[:], gtmp[:], recb[:])
            return v.drain()
        emit("v", vepi, waits=[evw], inc="v")
        evg = ev("v")
        for b in PG:
            mark_rd(b, evg)
        evt2 = transpose_to(p0, g2[:], evg)
        emit("s", lambda s_: s_.copy(gfm[:], p0[:]), waits=[evt2], inc="s")
        egf = ev("s")
        mark_rd(p0, egf)
        evo = proj(gfm, "wot", p5, p6, egf)
        def vro(v):
            v.tensor_add(R[:, 0:512], R[:, 0:512], p5[:])
            return v.tensor_add(R[:, 512:1024], R[:, 512:1024], p6[:])
        emit("v", vro, waits=[evo], inc="v")
        e = ev("v")
        mark_rd(p5, e); mark_rd(p6, e)
        return e

    for i in range(n_tiles):
        R = Rb[i % 2]
        subl = _CFG["sublayers"]
        if "F1" in subl:
            ffn(i, R, "F1", first=(subl[0] == "F1"))
        if "A" in subl:
            attn(i, R, first=(subl[0] == "A"))
        if "F2" in subl:
            ffn(i, R, "F2", first=(subl[0] == "F2"))
        def stor(s_, i=i):
            return s_.dma_start(out_d[i * TKN:(i + 1) * TKN, :], Rb[i % 2][:])
        emit("s", lambda s_, i=i: stor(s_, i).then_inc(semDO, 16),
             waits=[wV()])

    ops["y"].append(([(semDO, 16 * n_tiles)], lambda y: y.nop(), None))

    with nc.Block() as block:
        @block.tensor
        def _(tensor):
            for ws, fn, inc in ops["t"]:
                for s_, v_ in ws:
                    tensor.wait_ge(s_, v_)
                inst = fn(tensor)
                if inc is not None and inst is not None:
                    inst.then_inc(SEMO[inc], 1)

        @block.vector
        def _(vector):
            for ws, fn, inc in ops["v"]:
                for s_, v_ in ws:
                    vector.wait_ge(s_, v_)
                inst = fn(vector)
                if inc is not None and inst is not None:
                    inst.then_inc(SEMO[inc], 1)

        @block.scalar
        def _(scalar):
            for ws, fn, inc in ops["s"]:
                for s_, v_ in ws:
                    scalar.wait_ge(s_, v_)
                inst = fn(scalar)
                if inc is not None and inst is not None:
                    inst.then_inc(SEMO[inc], 1)

        @block.sync
        def _(sync):
            for ws, fn, inc in ops["y"]:
                for s_, v_ in ws:
                    sync.wait_ge(s_, v_)
                fn(sync)

        @block.gpsimd
        def _(gpsimd):
            for ws, fn, inc in ops["g"]:
                for s_, v_ in ws:
                    gpsimd.wait_ge(s_, v_)
                fn(gpsimd)

    es.close()
    return nc


def kernel(L0_attn_x, L0_attn_mix_k, L0_attn_mix_v, L0_attn_mix_r,
           L0_attn_Wk, L0_attn_Wv, L0_attn_Wr, L0_attn_Wo,
           L0_attn_time_decay, L0_attn_time_first,
           L0_ffn_W_r, L0_ffn_W_k, L0_ffn_W_v, L0_ffn_mu_r, L0_ffn_mu_k):
    f32 = np.float32
    x = np.ascontiguousarray(np.asarray(L0_attn_x, dtype=f32))

    def wt(w):
        return np.ascontiguousarray(np.asarray(w, dtype=f32).T)

    weights = {
        "fwrt": wt(L0_ffn_W_r), "fwkt": wt(L0_ffn_W_k), "fwvt": wt(L0_ffn_W_v),
        "wkt": wt(L0_attn_Wk), "wvt": wt(L0_attn_Wv), "wrt": wt(L0_attn_Wr),
        "wot": wt(L0_attn_Wo),
    }

    def mutile(v):
        return np.ascontiguousarray(
            np.asarray(v, dtype=f32).reshape(-1).reshape(8, 128).T)

    mus = {
        "murt": mutile(L0_ffn_mu_r), "mukt": mutile(L0_ffn_mu_k),
        "mxkt": mutile(L0_attn_mix_k), "mxvt": mutile(L0_attn_mix_v),
        "mxrt": mutile(L0_attn_mix_r),
    }
    td = np.asarray(L0_attn_time_decay, dtype=f32).reshape(H)
    tf = np.asarray(L0_attn_time_first, dtype=f32).reshape(H)
    tdr = np.ascontiguousarray(np.broadcast_to(td[None, :], (128, H)))
    tfr = np.ascontiguousarray(np.broadcast_to(tf[None, :], (128, H)))

    j = np.arange(128, dtype=np.float64)
    consts = {
        "tri": (lambda t: (t.__setitem__((slice(None), 0), 1.0), t)[1])(
            np.triu(np.ones((128, 128), f32), 1)),
        "idf": np.eye(128, dtype=f32),
        "ones1": np.ones((1, 128), f32),
        "cju": (CCENT - j).astype(f32).reshape(128, 1),
        "cjw": (j - 1.0 - CCENT).astype(f32).reshape(128, 1),
        "tdr": tdr, "tfr": tfr,
    }

    nc = _build_nc()
    in_maps = []
    for c in range(NCORES):
        m = {"x": np.ascontiguousarray(
            x[2 * c:2 * c + 2].reshape(ROWS, C))}
        m.update(weights)
        m.update(mus)
        m.update(consts)
        in_maps.append(m)

    res = run_bass_kernel_spmd(nc, in_maps, core_ids=list(range(NCORES)))
    out = np.stack([np.asarray(r["out"], dtype=f32) for r in res.results])
    return out.reshape(16, 2048, 1024)


# version 02
# speedup vs baseline: 1.1353x; 1.1353x over previous
import contextlib
import numpy as np
import ml_dtypes
import concourse.bass as bass
import concourse.mybir as mybir
from concourse.bass_utils import run_bass_kernel_spmd

F32 = mybir.dt.float32
BF16 = mybir.dt.bfloat16
AF = mybir.ActivationFunctionType
ALU = mybir.AluOpType
AX = mybir.AxisListType

NCORES = 8
SEQ = 2048
C = 1024
H = 16
TKN = 128
ROWS = 4096
NT = ROWS // TKN
TPS = SEQ // TKN
CCENT = 63.5

_CFG = {"n_tiles": NT, "sublayers": ("F1", "A", "F2")}


def _build_nc():
    nc = bass.Bass("TRN2", target_bir_lowering=False)

    x_d = nc.dram_tensor("x", [ROWS, C], F32, kind="ExternalInput").ap()
    out_d = nc.dram_tensor("out", [ROWS, C], F32, kind="ExternalOutput").ap()
    WNAMES = ["fwrt", "fwkt", "fwvt", "wkt", "wvt", "wrt", "wot"]
    w_d = {n: nc.dram_tensor(n, [C, C], F32, kind="ExternalInput").ap()
           for n in WNAMES}
    tri_d = nc.dram_tensor("tri", [128, 128], F32, kind="ExternalInput").ap()
    idf_d = nc.dram_tensor("idf", [128, 128], F32, kind="ExternalInput").ap()
    ones1_d = nc.dram_tensor("ones1", [1, 128], F32, kind="ExternalInput").ap()
    cju_d = nc.dram_tensor("cju", [128, 1], F32, kind="ExternalInput").ap()
    cjw_d = nc.dram_tensor("cjw", [128, 1], F32, kind="ExternalInput").ap()
    tdr_d = nc.dram_tensor("tdr", [128, H], F32, kind="ExternalInput").ap()
    tfr_d = nc.dram_tensor("tfr", [128, H], F32, kind="ExternalInput").ap()
    mu_d = {n: nc.dram_tensor(n, [128, 8], F32, kind="ExternalInput").ap()
            for n in ["murt", "mukt", "mxkt", "mxvt", "mxrt"]}

    es = contextlib.ExitStack()
    sb = lambda name, shape, dt: es.enter_context(nc.sbuf_tensor(name, shape, dt))
    ps = lambda name, shape, dt: es.enter_context(nc.psum_tensor(name, shape, dt))
    sem = lambda name: es.enter_context(nc.semaphore(name))

    w_sb = {n: sb(n + "_s", [128, 8 * C], BF16) for n in WNAMES}
    Rb = [sb("R0", [128, C], F32), sb("R1", [128, C], F32)]
    z_tm = sb("z_tm", [128, C], BF16)
    zf = {s: sb("zf" + s, [128, 8 * 129], BF16) for s in ("F1", "A", "F2")}
    dbuf = sb("dbuf", [128, 8 * 128], BF16)
    xr_b = sb("xr", [128, 8 * 128], BF16)
    xk_b = sb("xk", [128, 8 * 128], BF16)
    xv_b = sb("xv", [128, 8 * 128], BF16)
    junk = sb("junk", [128, C], BF16)
    s1 = sb("s1", [128, 1], F32); s2 = sb("s2", [128, 1], F32)
    meanb = sb("meanb", [128, 1], F32); varb = sb("varb", [128, 1], F32)
    m2b = sb("m2b", [128, 1], F32); lnvb = sb("lnvb", [128, 1], F32)
    invb = sb("invb", [128, 1], F32); negmib = sb("negmib", [128, 1], F32)
    krelu = sb("krelu", [128, C], BF16)
    ka_tm = sb("ka_tm", [128, C], BF16)
    ka_fm = sb("ka_fm", [128, 8 * 128], BF16)
    esig = sb("esig", [128, C], F32)
    denb = sb("denb", [128, C], F32)
    recb = sb("recb", [128, C], F32)
    lnvb2 = sb("lnvb2", [128, C], F32)
    u_sb = sb("u_sb", [128, C], F32)
    ek = sb("ek", [128, C], F32)
    EKV = sb("EKV", [128, 1040], F32)
    ahat = sb("ahat", [128, 1040], F32)
    t2b = sb("t2b", [128, 1040], F32)
    tmpb = sb("tmpb", [128, 1040], F32)
    numb = sb("numb", [128, 1040], F32)
    qrow = sb("qrow", [1, 1040], F32)
    Aprime = sb("Aprime", [1, 1040], F32)
    gtmp = sb("gtmp", [128, C], F32)
    g2 = sb("g2", [128, C], BF16)
    recip16 = sb("recip16", [128, H], F32)
    r16ln = sb("r16ln", [128, H], F32)
    gfm = sb("gfm", [128, 8 * 128], BF16)
    tri = sb("tri_s", [128, 128], F32)
    idf = sb("idf_s", [128, 128], F32)
    idb = sb("idb_s", [128, 128], BF16)
    ones1 = sb("ones1_s", [1, 128], F32)
    cju = sb("cju_s", [128, 1], F32)
    cjw = sb("cjw_s", [128, 1], F32)
    tdr = sb("tdr_s", [128, H], F32)
    tfr = sb("tfr_s", [128, H], F32)
    wtil = sb("wtil", [128, H], F32)
    negw = sb("negw", [128, H], F32)
    utab = sb("utab", [128, H], F32)
    wtab = sb("wtab", [128, H], F32)
    ew128 = sb("ew128", [128, H], F32)
    eu16 = sb("eu16", [128, H], F32)
    mu_sb = {n: sb(n + "_s", [128, 8], F32) for n in mu_d}

    p0 = ps("p0", [128, 1024], BF16)
    pg = [ps(f"p{k}", [128, 512], F32) for k in range(1, 7)]
    p1, p2, p3, p4, p5, p6 = pg

    semT = sem("semT"); semV = sem("semV"); semS = sem("semS")
    semDX = sem("semDX"); semDO = sem("semDO")
    semDW = sem("semDW"); semDC = sem("semDC")

    ops = {k: [] for k in ("t", "v", "s", "y", "g")}
    cnt = {"t": 0, "v": 0, "s": 0}
    SEMO = {"t": semT, "v": semV, "s": semS}

    def emit(stream, fn, waits=(), inc=None):
        ws = [(s_, v_) for (s_, v_) in waits if v_ > 0]
        ops[stream].append((ws, fn, inc))
        if inc is not None:
            cnt[inc] += 1

    def wT(): return (semT, cnt["t"])
    def wV(): return (semV, cnt["v"])
    def wS(): return (semS, cnt["s"])

    def ev(stream):
        return (SEMO[stream], cnt[stream])

    w3 = {n: w_sb[n][:].rearrange("p (cc d) -> p cc d", cc=8) for n in WNAMES}
    zf3 = {s: zf[s][:].rearrange("p (cc t) -> p cc t", cc=8) for s in zf}
    EKVv = EKV[:].rearrange("p (h c) -> p h c", h=H)
    ahv = ahat[:].rearrange("p (h c) -> p h c", h=H)
    t2v = t2b[:].rearrange("p (h c) -> p h c", h=H)
    tmpv = tmpb[:].rearrange("p (h c) -> p h c", h=H)
    numv = numb[:].rearrange("p (h c) -> p h c", h=H)
    qrv = qrow[:].rearrange("p (h c) -> p h c", h=H)
    Apv = Aprime[:].rearrange("p (h c) -> p h c", h=H)
    ekv16 = ek[:].rearrange("p (h d) -> p h d", h=H)
    gtv = gtmp[:].rearrange("p (h d) -> p h d", h=H)

    def bc(ap2d, n65, parts=128):
        return ap2d.unsqueeze(2).broadcast_to([parts, ap2d.shape[1], n65])

    def pro_y(sync):
        for nm, dst, src in [("tri", tri, tri_d), ("idf", idf, idf_d),
                             ("ones1", ones1, ones1_d), ("cju", cju, cju_d),
                             ("cjw", cjw, cjw_d), ("tdr", tdr, tdr_d),
                             ("tfr", tfr, tfr_d)]:
            sync.dma_start(dst[:], src).then_inc(semDC, 16)
        for nm in mu_d:
            sync.dma_start(mu_sb[nm][:], mu_d[nm]).then_inc(semDC, 16)
    ops["y"].append(([], pro_y, None))
    NCONST = 12

    def pro_g(gpsimd):
        for nm in WNAMES:
            src = w_d[nm].rearrange("(cc p) d -> p cc d", p=128)
            gpsimd.dma_start(w3[nm], src).then_inc(semDW, 16)
    ops["g"].append(([], pro_g, None))
    WIDX = {n: 16 * (k + 1) for k, n in enumerate(WNAMES)}

    emit("s", lambda s_: s_.activation(wtil[:], tdr[:], AF.Exp),
         waits=[(semDC, 16 * NCONST)], inc="s")
    emit("v", lambda v: v.tensor_scalar(negw[:], wtil[:], -1.0, None, ALU.mult),
         waits=[wS()], inc="v")

    def tabs(s_):
        s_.activation(utab[:], negw[:], AF.Exp, scale=cju[:])
        s_.activation(wtab[:], negw[:], AF.Exp, scale=cjw[:])
        s_.activation(ew128[:], negw[:], AF.Exp, scale=128.0)
        return s_.activation(eu16[:], tfr[:], AF.Exp)
    emit("s", tabs, waits=[wV()], inc="s")

    def pro_v(v):
        v.tensor_copy(idb[:], idf[:])
        v.tensor_scalar(w_sb["fwvt"][:], w_sb["fwvt"][:], 0.5, None, ALU.mult)
        v.memset(Aprime[:], 0.0)
        for s in zf3:
            v.memset(zf3[s][:, :, 0:1], 0.0)
        return v.nop()
    emit("v", pro_v, waits=[(semDW, WIDX["fwvt"]), (semDC, 16 * NCONST)],
         inc="v")
    EV_PRO_V = ev("v")

    n_tiles = _CFG["n_tiles"]
    for i in range(n_tiles):
        def load(sync, i=i):
            return sync.dma_start(Rb[i % 2][:], x_d[i * TKN:(i + 1) * TKN, :])
        waits = [(semDO, 16 * (i - 1))] if i >= 2 else []
        ops["y"].append((waits, lambda y, i=i: load(y, i).then_inc(semDX, 16),
                         None))

    last_rd = {id(p0): None}
    for p in pg:
        last_rd[id(p)] = None

    def rd_wait(*banks):
        ws = []
        for b in banks:
            e = last_rd[id(b)]
            if e is not None:
                ws.append(e)
        return ws

    def mark_rd(bank, event):
        last_rd[id(bank)] = event

    def layer_norm(R, i, first):
        w0 = [(semDX, 16 * (i + 1))] if first else []
        def v1(v):
            v.reduce_sum(s1[:], R[:], axis=AX.X)
            v.drain()
            return v.tensor_scalar(meanb[:], s1[:], 1.0 / C, None, ALU.mult)
        emit("v", v1, waits=w0 + [wS(), wT()], inc="v")
        emit("s", lambda s_: s_.activation(junk[:], R[:], AF.Square,
                                           accum_out=s2[:]),
             waits=w0 + [wV()], inc="s")
        def v2(v):
            v.drain()
            v.tensor_scalar(varb[:], s2[:], 1.0 / C, None, ALU.mult)
            v.tensor_scalar(m2b[:], meanb[:], meanb[:], None, ALU.mult)
            v.drain()
            v.tensor_sub(varb[:], varb[:], m2b[:])
            v.drain()
            return v.tensor_scalar(varb[:], varb[:], 1e-5, None, ALU.add)
        emit("v", v2, waits=[wS()], inc="v")
        def s2f(s_):
            s_.activation(lnvb[:], varb[:], AF.Ln)
            s_.drain()
            return s_.activation(invb[:], lnvb[:], AF.Exp, scale=-0.5)
        emit("s", s2f, waits=[wV()], inc="s")
        def v3(v):
            v.tensor_scalar(negmib[:], invb[:], meanb[:], -1.0,
                            ALU.mult, ALU.mult)
            v.drain()
            return v.tensor_scalar(z_tm[:], R[:], invb[:], negmib[:],
                                   ALU.mult, ALU.add)
        emit("v", v3, waits=[wS()], inc="v")
        return ev("v")

    def transpose_to(p0bank, src, ev_src):
        def tf(t):
            last = None
            for cc in range(8):
                last = t.transpose(p0bank[:, cc * 128:(cc + 1) * 128],
                                   src[:, cc * 128:(cc + 1) * 128], idb[:])
            return last
        emit("t", tf, waits=[ev_src, EV_PRO_V] + rd_wait(p0bank), inc="t")
        return ev("t")

    def evac_z(slayer, ev_t):
        emit("s", lambda s_: s_.copy(
            zf3[slayer][:, :, 1:129],
            p0[:].rearrange("p (cc t) -> p cc t", cc=8)),
            waits=[ev_t], inc="s")
        e = ev("s")
        mark_rd(p0, e)
        return e

    def mixes(slayer, i, targets, ev_evac):
        z3 = zf3[slayer]
        n = i % TPS
        def vm(v):
            if n == 0 and i > 0:
                v.memset(z3[:, :, 0:1], 0.0)
            v.tensor_sub(dbuf[:].rearrange("p (cc t) -> p cc t", cc=8),
                         z3[:, :, 1:129], z3[:, :, 0:128])
            v.drain()
            for (dst, mu) in targets:
                for cc in range(8):
                    v.scalar_tensor_tensor(
                        dst[:, cc * 128:(cc + 1) * 128],
                        dbuf[:, cc * 128:(cc + 1) * 128],
                        mu_sb[mu][:, cc:cc + 1],
                        zf[slayer][:, cc * 129:cc * 129 + 128],
                        ALU.mult, ALU.add)
            v.drain()
            return v.tensor_copy(z3[:, :, 0:1], z3[:, :, 128:129])
        emit("v", vm, waits=[ev_evac, wT()], inc="v")
        return ev("v")

    def proj(lhsbuf, wname, pa, pb, ev_mix, extra_w=()):
        wv = w3[wname]
        def tp(t):
            last = None
            for cc in range(8):
                st = cc == 0
                sp = cc == 7
                t.matmul(pa[:], lhsbuf[:, cc * 128:(cc + 1) * 128],
                         wv[:, cc, 0:512], start=st, stop=sp)
                last = t.matmul(pb[:], lhsbuf[:, cc * 128:(cc + 1) * 128],
                                wv[:, cc, 512:1024], start=st, stop=sp)
            return last
        waits = [ev_mix, (semDW, WIDX[wname]), EV_PRO_V] + rd_wait(pa, pb) \
            + list(extra_w)
        emit("t", tp, waits=waits, inc="t")
        return ev("t")

    def ffn(i, R, slayer, first):
        evz = layer_norm(R, i, first=first)
        evt = transpose_to(p0, z_tm[:], evz)
        evev = evac_z(slayer, evt)
        evm = mixes(slayer, i, [(xr_b, "murt"), (xk_b, "mukt")], evev)
        evr = proj(xr_b, "fwrt", p1, p2, evm)
        evk = proj(xk_b, "fwkt", p3, p4, evm)
        def sact(s_):
            s_.activation(krelu[:, 0:512], p3[:], AF.Relu)
            s_.activation(krelu[:, 512:1024], p4[:], AF.Relu)
            s_.activation(esig[:, 0:512], p1[:], AF.Exp, scale=-1.0)
            return s_.activation(esig[:, 512:1024], p2[:], AF.Exp, scale=-1.0)
        emit("s", sact, waits=[evk], inc="s")
        e = ev("s")
        mark_rd(p1, e); mark_rd(p2, e); mark_rd(p3, e); mark_rd(p4, e)
        def vka(v):
            v.tensor_mul(ka_tm[:], krelu[:], krelu[:])
            v.tensor_scalar(denb[:], esig[:], 1.0, None, ALU.add)
            return v.drain()
        emit("v", vka, waits=[wS()], inc="v")
        evka = ev("v")
        def srec(s_):
            s_.activation(lnvb2[:], denb[:], AF.Ln)
            s_.drain()
            return s_.activation(recb[:], lnvb2[:], AF.Exp, scale=-1.0)
        emit("s", srec, waits=[evka], inc="s")
        evrec = ev("s")
        evt2 = transpose_to(p0, ka_tm[:], evka)
        emit("s", lambda s_: s_.copy(ka_fm[:], p0[:]), waits=[evt2], inc="s")
        eevac = ev("s")
        mark_rd(p0, eevac)
        evq = proj(ka_fm, "fwvt", p3, p4, eevac)
        def vu(v):
            v.tensor_mul(u_sb[:, 0:512], recb[:, 0:512], p3[:])
            v.tensor_mul(u_sb[:, 512:1024], recb[:, 512:1024], p4[:])
            v.drain()
            return v.tensor_add(R[:], R[:], u_sb[:])
        wsto = [(semDO, 16 * (i - 1))] if i >= 2 else []
        emit("v", vu, waits=[evq, evrec] + wsto, inc="v")
        e = ev("v")
        mark_rd(p3, e); mark_rd(p4, e)
        return e

    def attn(i, R, first):
        n = i % TPS
        evz = layer_norm(R, i, first=first)
        evt = transpose_to(p0, z_tm[:], evz)
        evev = evac_z("A", evt)
        evm = mixes("A", i, [(xk_b, "mxkt"), (xv_b, "mxvt"), (xr_b, "mxrt")],
                    evev)
        evk = proj(xk_b, "wkt", p1, p2, evm)
        evv = proj(xv_b, "wvt", p3, p4, evm)
        evr = proj(xr_b, "wrt", p5, p6, evm)
        def sact(s_):
            s_.activation(ek[:, 0:512], p1[:], AF.Exp)
            s_.activation(ek[:, 512:1024], p2[:], AF.Exp)
            s_.activation(esig[:, 0:512], p5[:], AF.Exp, scale=-1.0)
            return s_.activation(esig[:, 512:1024], p6[:], AF.Exp, scale=-1.0)
        emit("s", sact, waits=[evr], inc="s")
        es_ = ev("s")
        mark_rd(p1, es_); mark_rd(p2, es_); mark_rd(p5, es_); mark_rd(p6, es_)
        def vek(v):
            if n == 0 and i > 0:
                v.memset(Aprime[:], 0.0)
            v.tensor_mul(EKVv[:, 0:8, 0:64],
                         ek[:, 0:512].rearrange("p (h d) -> p h d", h=8),
                         p3[:].rearrange("p (h d) -> p h d", h=8))
            v.tensor_mul(EKVv[:, 8:16, 0:64],
                         ek[:, 512:1024].rearrange("p (h d) -> p h d", h=8),
                         p4[:].rearrange("p (h d) -> p h d", h=8))
            v.reduce_sum(EKVv[:, :, 64], ekv16, axis=AX.X)
            v.drain()
            v.tensor_mul(ahv, EKVv, bc(utab[:], 65))
            v.tensor_scalar(denb[:], esig[:], 1.0, None, ALU.add)
            return v.drain()
        emit("v", vek, waits=[wS()], inc="v")
        evek = ev("v")
        mark_rd(p3, evek); mark_rd(p4, evek)
        def sreca(s_):
            s_.activation(lnvb2[:], denb[:], AF.Ln)
            s_.drain()
            return s_.activation(recb[:], lnvb2[:], AF.Exp, scale=-1.0)
        emit("s", sreca, waits=[evek], inc="s")
        evreca = ev("s")
        PG = [p1, p2, p3, p4]
        def twkv(t):
            last = None
            for g in range(4):
                t.matmul(PG[g][:, 0:260], tri[:],
                         ahat[:, 260 * g:260 * (g + 1)],
                         start=True, stop=False)
                last = t.matmul(PG[g][:, 0:260], ones1[:],
                                Aprime[0:1, 260 * g:260 * (g + 1)],
                                start=False, stop=True)
            return last
        emit("t", twkv, waits=[evek] + rd_wait(p1, p2, p3, p4), inc="t")
        evw = ev("t")
        def vepi(v):
            for g in range(4):
                v.tensor_mul(t2v[:, 4 * g:4 * (g + 1), :],
                             PG[g][:, 0:260].rearrange("p (h c) -> p h c", h=4),
                             bc(wtab[:, 4 * g:4 * (g + 1)], 65))
            v.tensor_mul(tmpv, EKVv, bc(eu16[:], 65))
            v.tensor_mul(t2v[0:1, :, :], Apv,
                         bc(wtab[0:1, :], 65, parts=1))
            v.drain()
            v.tensor_scalar(tmpv[:, :, 64], tmpv[:, :, 64], 1e-6, None,
                            ALU.max)
            v.drain()
            v.tensor_add(numb[:], t2b[:], tmpb[:])
            for g in range(4):
                v.tensor_mul(
                    Apv[0:1, 4 * g:4 * (g + 1), :],
                    PG[g][0:1, 0:260].rearrange("p (h c) -> p h c", h=4),
                    bc(ew128[0:1, 4 * g:4 * (g + 1)], 65, parts=1))
            return v.drain()
        emit("v", vepi, waits=[evw], inc="v")
        evn = ev("v")
        for b in PG:
            mark_rd(b, evn)
        def srec16(s_):
            s_.activation(r16ln[:], numv[:, :, 64], AF.Ln)
            s_.drain()
            return s_.activation(recip16[:], r16ln[:], AF.Exp, scale=-1.0)
        emit("s", srec16, waits=[evn], inc="s")
        evr16 = ev("s")
        def vepi2(v):
            v.tensor_mul(gtv, numv[:, :, 0:64], bc(recip16[:], 64))
            v.drain()
            v.tensor_mul(g2[:], gtmp[:], recb[:])
            return v.drain()
        emit("v", vepi2, waits=[evr16, evreca], inc="v")
        evg = ev("v")
        evt2 = transpose_to(p0, g2[:], evg)
        emit("s", lambda s_: s_.copy(gfm[:], p0[:]), waits=[evt2], inc="s")
        egf = ev("s")
        mark_rd(p0, egf)
        evo = proj(gfm, "wot", p5, p6, egf)
        def vro(v):
            v.tensor_add(R[:, 0:512], R[:, 0:512], p5[:])
            return v.tensor_add(R[:, 512:1024], R[:, 512:1024], p6[:])
        emit("v", vro, waits=[evo], inc="v")
        e = ev("v")
        mark_rd(p5, e); mark_rd(p6, e)
        return e

    for i in range(n_tiles):
        R = Rb[i % 2]
        subl = _CFG["sublayers"]
        if "F1" in subl:
            ffn(i, R, "F1", first=(subl[0] == "F1"))
        if "A" in subl:
            attn(i, R, first=(subl[0] == "A"))
        if "F2" in subl:
            ffn(i, R, "F2", first=(subl[0] == "F2"))
        def stor(s_, i=i):
            return s_.dma_start(out_d[i * TKN:(i + 1) * TKN, :], Rb[i % 2][:])
        emit("s", lambda s_, i=i: stor(s_, i).then_inc(semDO, 16),
             waits=[wV()])

    ops["y"].append(([(semDO, 16 * n_tiles)], lambda y: y.nop(), None))

    with nc.Block() as block:
        @block.tensor
        def _(tensor):
            for ws, fn, inc in ops["t"]:
                for s_, v_ in ws:
                    tensor.wait_ge(s_, v_)
                inst = fn(tensor)
                if inc is not None and inst is not None:
                    inst.then_inc(SEMO[inc], 1)

        @block.vector
        def _(vector):
            for ws, fn, inc in ops["v"]:
                for s_, v_ in ws:
                    vector.wait_ge(s_, v_)
                inst = fn(vector)
                if inc is not None and inst is not None:
                    inst.then_inc(SEMO[inc], 1)

        @block.scalar
        def _(scalar):
            for ws, fn, inc in ops["s"]:
                for s_, v_ in ws:
                    scalar.wait_ge(s_, v_)
                inst = fn(scalar)
                if inc is not None and inst is not None:
                    inst.then_inc(SEMO[inc], 1)

        @block.sync
        def _(sync):
            for ws, fn, inc in ops["y"]:
                for s_, v_ in ws:
                    sync.wait_ge(s_, v_)
                fn(sync)

        @block.gpsimd
        def _(gpsimd):
            for ws, fn, inc in ops["g"]:
                for s_, v_ in ws:
                    gpsimd.wait_ge(s_, v_)
                fn(gpsimd)

    es.close()
    return nc


def kernel(L0_attn_x, L0_attn_mix_k, L0_attn_mix_v, L0_attn_mix_r,
           L0_attn_Wk, L0_attn_Wv, L0_attn_Wr, L0_attn_Wo,
           L0_attn_time_decay, L0_attn_time_first,
           L0_ffn_W_r, L0_ffn_W_k, L0_ffn_W_v, L0_ffn_mu_r, L0_ffn_mu_k):
    f32 = np.float32
    x = np.ascontiguousarray(np.asarray(L0_attn_x, dtype=f32))

    def wt(w):
        return np.ascontiguousarray(np.asarray(w, dtype=f32).T)

    weights = {
        "fwrt": wt(L0_ffn_W_r), "fwkt": wt(L0_ffn_W_k), "fwvt": wt(L0_ffn_W_v),
        "wkt": wt(L0_attn_Wk), "wvt": wt(L0_attn_Wv), "wrt": wt(L0_attn_Wr),
        "wot": wt(L0_attn_Wo),
    }

    def mutile(v):
        return np.ascontiguousarray(
            np.asarray(v, dtype=f32).reshape(-1).reshape(8, 128).T)

    mus = {
        "murt": mutile(L0_ffn_mu_r), "mukt": mutile(L0_ffn_mu_k),
        "mxkt": mutile(L0_attn_mix_k), "mxvt": mutile(L0_attn_mix_v),
        "mxrt": mutile(L0_attn_mix_r),
    }
    td = np.asarray(L0_attn_time_decay, dtype=f32).reshape(H)
    tf = np.asarray(L0_attn_time_first, dtype=f32).reshape(H)
    tdr = np.ascontiguousarray(np.broadcast_to(td[None, :], (128, H)))
    tfr = np.ascontiguousarray(np.broadcast_to(tf[None, :], (128, H)))

    j = np.arange(128, dtype=np.float64)
    consts = {
        "tri": (lambda t: (t.__setitem__((slice(None), 0), 1.0), t)[1])(
            np.triu(np.ones((128, 128), f32), 1)),
        "idf": np.eye(128, dtype=f32),
        "ones1": np.ones((1, 128), f32),
        "cju": (CCENT - j).astype(f32).reshape(128, 1),
        "cjw": (j - 1.0 - CCENT).astype(f32).reshape(128, 1),
        "tdr": tdr, "tfr": tfr,
    }

    nc = _build_nc()
    in_maps = []
    for c in range(NCORES):
        m = {"x": np.ascontiguousarray(
            x[2 * c:2 * c + 2].reshape(ROWS, C))}
        m.update(weights)
        m.update(mus)
        m.update(consts)
        in_maps.append(m)

    res = run_bass_kernel_spmd(nc, in_maps, core_ids=list(range(NCORES)))
    out = np.stack([np.asarray(r["out"], dtype=f32) for r in res.results])
    return out.reshape(16, 2048, 1024)


# version 03
# speedup vs baseline: 1.1581x; 1.0201x over previous
import contextlib
import numpy as np
import ml_dtypes
import concourse.bass as bass
import concourse.mybir as mybir
from concourse.bass_utils import run_bass_kernel_spmd

F32 = mybir.dt.float32
BF16 = mybir.dt.bfloat16
AF = mybir.ActivationFunctionType
ALU = mybir.AluOpType
AX = mybir.AxisListType

NCORES = 8
SEQ = 2048
C = 1024
H = 16
TKN = 128
ROWS = 4096
NT = ROWS // TKN
TPS = SEQ // TKN
CCENT = 63.5

_CFG = {"n_tiles": NT, "sublayers": ("F1", "A", "F2")}


def _build_nc():
    nc = bass.Bass("TRN2", target_bir_lowering=False)

    x_d = nc.dram_tensor("x", [ROWS, C], F32, kind="ExternalInput").ap()
    out_d = nc.dram_tensor("out", [ROWS, C], F32, kind="ExternalOutput").ap()
    WNAMES = ["fwrt", "fwkt", "fwvt", "wkt", "wvt", "wrt", "wot"]
    w_d = {n: nc.dram_tensor(n, [C, C], F32, kind="ExternalInput").ap()
           for n in WNAMES}
    tri_d = nc.dram_tensor("tri", [128, 128], F32, kind="ExternalInput").ap()
    idf_d = nc.dram_tensor("idf", [128, 128], F32, kind="ExternalInput").ap()
    ones1_d = nc.dram_tensor("ones1", [1, 128], F32, kind="ExternalInput").ap()
    cju_d = nc.dram_tensor("cju", [128, 1], F32, kind="ExternalInput").ap()
    cjw_d = nc.dram_tensor("cjw", [128, 1], F32, kind="ExternalInput").ap()
    tdr_d = nc.dram_tensor("tdr", [128, H], F32, kind="ExternalInput").ap()
    tfr_d = nc.dram_tensor("tfr", [128, H], F32, kind="ExternalInput").ap()
    mu_d = {n: nc.dram_tensor(n, [128, 8], F32, kind="ExternalInput").ap()
            for n in ["murt", "mukt", "mxkt", "mxvt", "mxrt"]}

    es = contextlib.ExitStack()
    sb = lambda name, shape, dt: es.enter_context(nc.sbuf_tensor(name, shape, dt))
    ps = lambda name, shape, dt: es.enter_context(nc.psum_tensor(name, shape, dt))
    sem = lambda name: es.enter_context(nc.semaphore(name))

    w_sb = {n: sb(n + "_s", [128, 8 * C], BF16) for n in WNAMES}
    Rb = [sb("R0", [128, C], F32), sb("R1", [128, C], F32)]
    z_tm = sb("z_tm", [128, C], BF16)
    zf = {s: sb("zf" + s, [128, 8 * 129], BF16) for s in ("F1", "A", "F2")}
    dbuf = sb("dbuf", [128, 8 * 128], BF16)
    xr_b = sb("xr", [128, 8 * 128], BF16)
    xk_b = sb("xk", [128, 8 * 128], BF16)
    xv_b = sb("xv", [128, 8 * 128], BF16)
    junk = sb("junk", [128, C], BF16)
    s1 = sb("s1", [128, 1], F32); s2 = sb("s2", [128, 1], F32)
    meanb = sb("meanb", [128, 1], F32); varb = sb("varb", [128, 1], F32)
    m2b = sb("m2b", [128, 1], F32); lnvb = sb("lnvb", [128, 1], F32)
    invb = sb("invb", [128, 1], F32); negmib = sb("negmib", [128, 1], F32)
    krelu = sb("krelu", [128, C], BF16)
    ka_tm = sb("ka_tm", [128, C], BF16)
    ka_fm = sb("ka_fm", [128, 8 * 128], BF16)
    esig = sb("esig", [128, C], F32)
    denb = sb("denb", [128, C], F32)
    recb = sb("recb", [128, C], F32)
    lnvb2 = sb("lnvb2", [128, C], F32)
    u_sb = sb("u_sb", [128, C], F32)
    ek = sb("ek", [128, C], F32)
    EKV = sb("EKV", [128, 1040], F32)
    ahat = sb("ahat", [128, 1040], F32)
    t2b = sb("t2b", [128, 1040], F32)
    tmpb = sb("tmpb", [128, 1040], F32)
    numb = sb("numb", [128, 1040], F32)
    qrow = sb("qrow", [1, 1040], F32)
    Aprime = sb("Aprime", [1, 1040], F32)
    gtmp = sb("gtmp", [128, C], F32)
    g2 = sb("g2", [128, C], BF16)
    recip16 = sb("recip16", [128, H], F32)
    r16ln = sb("r16ln", [128, H], F32)
    gfm = sb("gfm", [128, 8 * 128], BF16)
    tri = sb("tri_s", [128, 128], F32)
    idf = sb("idf_s", [128, 128], F32)
    idb = sb("idb_s", [128, 128], BF16)
    ones1 = sb("ones1_s", [1, 128], F32)
    cju = sb("cju_s", [128, 1], F32)
    cjw = sb("cjw_s", [128, 1], F32)
    tdr = sb("tdr_s", [128, H], F32)
    tfr = sb("tfr_s", [128, H], F32)
    wtil = sb("wtil", [128, H], F32)
    negw = sb("negw", [128, H], F32)
    utab = sb("utab", [128, H], F32)
    wtab = sb("wtab", [128, H], F32)
    ew128 = sb("ew128", [128, H], F32)
    eu16 = sb("eu16", [128, H], F32)
    mu_sb = {n: sb(n + "_s", [128, 8], F32) for n in mu_d}

    p0 = ps("p0", [128, 1024], BF16)
    pg = [ps(f"p{k}", [128, 512], F32) for k in range(1, 7)]
    p1, p2, p3, p4, p5, p6 = pg

    semT = sem("semT"); semV = sem("semV"); semS = sem("semS")
    semDX = sem("semDX"); semDO = sem("semDO")
    semDW = sem("semDW"); semDC = sem("semDC")

    ops = {k: [] for k in ("t", "v", "s", "y", "g")}
    cnt = {"t": 0, "v": 0, "s": 0}
    SEMO = {"t": semT, "v": semV, "s": semS}

    def emit(stream, fn, waits=(), inc=None):
        ws = [(s_, v_) for (s_, v_) in waits if v_ > 0]
        ops[stream].append((ws, fn, inc))
        if inc is not None:
            cnt[inc] += 1

    def wT(): return (semT, cnt["t"])
    def wV(): return (semV, cnt["v"])
    def wS(): return (semS, cnt["s"])

    def ev(stream):
        return (SEMO[stream], cnt[stream])

    w3 = {n: w_sb[n][:].rearrange("p (cc d) -> p cc d", cc=8) for n in WNAMES}
    zf3 = {s: zf[s][:].rearrange("p (cc t) -> p cc t", cc=8) for s in zf}
    EKVv = EKV[:].rearrange("p (h c) -> p h c", h=H)
    ahv = ahat[:].rearrange("p (h c) -> p h c", h=H)
    t2v = t2b[:].rearrange("p (h c) -> p h c", h=H)
    tmpv = tmpb[:].rearrange("p (h c) -> p h c", h=H)
    numv = numb[:].rearrange("p (h c) -> p h c", h=H)
    qrv = qrow[:].rearrange("p (h c) -> p h c", h=H)
    Apv = Aprime[:].rearrange("p (h c) -> p h c", h=H)
    ekv16 = ek[:].rearrange("p (h d) -> p h d", h=H)
    gtv = gtmp[:].rearrange("p (h d) -> p h d", h=H)

    def bc(ap2d, n65, parts=128):
        return ap2d.unsqueeze(2).broadcast_to([parts, ap2d.shape[1], n65])

    def pro_y(sync):
        for nm, dst, src in [("tri", tri, tri_d), ("idf", idf, idf_d),
                             ("ones1", ones1, ones1_d), ("cju", cju, cju_d),
                             ("cjw", cjw, cjw_d), ("tdr", tdr, tdr_d),
                             ("tfr", tfr, tfr_d)]:
            sync.dma_start(dst[:], src).then_inc(semDC, 16)
        for nm in mu_d:
            sync.dma_start(mu_sb[nm][:], mu_d[nm]).then_inc(semDC, 16)
    ops["y"].append(([], pro_y, None))
    NCONST = 12

    def pro_g(gpsimd):
        for nm in WNAMES:
            src = w_d[nm].rearrange("(cc p) d -> p cc d", p=128)
            gpsimd.dma_start(w3[nm], src).then_inc(semDW, 16)
    ops["g"].append(([], pro_g, None))
    WIDX = {n: 16 * (k + 1) for k, n in enumerate(WNAMES)}

    emit("s", lambda s_: s_.activation(wtil[:], tdr[:], AF.Exp),
         waits=[(semDC, 16 * NCONST)], inc="s")
    emit("v", lambda v: v.tensor_scalar(negw[:], wtil[:], -1.0, None, ALU.mult),
         waits=[wS()], inc="v")

    def tabs(s_):
        s_.activation(utab[:], negw[:], AF.Exp, scale=cju[:])
        s_.activation(wtab[:], negw[:], AF.Exp, scale=cjw[:])
        s_.activation(ew128[:], negw[:], AF.Exp, scale=128.0)
        return s_.activation(eu16[:], tfr[:], AF.Exp)
    emit("s", tabs, waits=[wV()], inc="s")

    def pro_v(v):
        v.tensor_copy(idb[:], idf[:])
        v.tensor_scalar(w_sb["fwvt"][:], w_sb["fwvt"][:], 0.5, None, ALU.mult)
        v.memset(Aprime[:], 0.0)
        for s in zf3:
            v.memset(zf3[s][:, :, 0:1], 0.0)
        return v.nop()
    emit("v", pro_v, waits=[(semDW, WIDX["fwvt"]), (semDC, 16 * NCONST)],
         inc="v")
    EV_PRO_V = ev("v")

    n_tiles = _CFG["n_tiles"]
    for i in range(n_tiles):
        def load(sync, i=i):
            return sync.dma_start(Rb[i % 2][:], x_d[i * TKN:(i + 1) * TKN, :])
        waits = [(semDO, 16 * (i - 1))] if i >= 2 else []
        ops["y"].append((waits, lambda y, i=i: load(y, i).then_inc(semDX, 16),
                         None))

    last_rd = {id(p0): None}
    for p in pg:
        last_rd[id(p)] = None

    def rd_wait(*banks):
        ws = []
        for b in banks:
            e = last_rd[id(b)]
            if e is not None:
                ws.append(e)
        return ws

    def mark_rd(bank, event):
        last_rd[id(bank)] = event

    track = {"v2": None, "zT": None,
             id(xr_b): None, id(xk_b): None, id(xv_b): None}

    def _w(*evs):
        return [e for e in evs if e is not None]

    def layer_norm(R, i, r_ev):
        def v1(v):
            v.reduce_sum(s1[:], R[:], axis=AX.X)
            v.drain()
            return v.tensor_scalar(meanb[:], s1[:], 1.0 / C, None, ALU.mult)
        emit("v", v1, waits=_w(r_ev), inc="v")
        emit("s", lambda s_: s_.activation(junk[:], R[:], AF.Square,
                                           accum_out=s2[:]),
             waits=_w(r_ev, track["v2"]), inc="s")
        ev_sq = ev("s")
        def v2(v):
            v.drain()
            v.tensor_scalar(varb[:], s2[:], 1.0 / C, None, ALU.mult)
            v.tensor_scalar(m2b[:], meanb[:], meanb[:], None, ALU.mult)
            v.drain()
            v.tensor_sub(varb[:], varb[:], m2b[:])
            v.drain()
            return v.tensor_scalar(varb[:], varb[:], 1e-5, None, ALU.add)
        emit("v", v2, waits=[ev_sq], inc="v")
        ev_v2 = ev("v")
        track["v2"] = ev_v2
        def s2f(s_):
            s_.activation(lnvb[:], varb[:], AF.Ln)
            s_.drain()
            return s_.activation(invb[:], lnvb[:], AF.Exp, scale=-0.5)
        emit("s", s2f, waits=[ev_v2], inc="s")
        ev_inv = ev("s")
        def v3(v):
            v.tensor_scalar(negmib[:], invb[:], meanb[:], -1.0,
                            ALU.mult, ALU.mult)
            v.drain()
            return v.tensor_scalar(z_tm[:], R[:], invb[:], negmib[:],
                                   ALU.mult, ALU.add)
        emit("v", v3, waits=_w(ev_inv, track["zT"]), inc="v")
        return ev("v")

    def transpose_to(p0bank, src, ev_src, is_z=False):
        def tf(t):
            last = None
            for cc in range(8):
                last = t.transpose(p0bank[:, cc * 128:(cc + 1) * 128],
                                   src[:, cc * 128:(cc + 1) * 128], idb[:])
            return last
        emit("t", tf, waits=[ev_src, EV_PRO_V] + rd_wait(p0bank), inc="t")
        e = ev("t")
        if is_z:
            track["zT"] = e
        return e

    def evac_z(slayer, ev_t):
        emit("s", lambda s_: s_.copy(
            zf3[slayer][:, :, 1:129],
            p0[:].rearrange("p (cc t) -> p cc t", cc=8)),
            waits=[ev_t], inc="s")
        e = ev("s")
        mark_rd(p0, e)
        return e

    def mixes(slayer, i, targets, ev_evac):
        z3 = zf3[slayer]
        n = i % TPS
        def vm(v):
            if n == 0 and i > 0:
                v.memset(z3[:, :, 0:1], 0.0)
            v.tensor_sub(dbuf[:].rearrange("p (cc t) -> p cc t", cc=8),
                         z3[:, :, 1:129], z3[:, :, 0:128])
            v.drain()
            for (dst, mu) in targets:
                for cc in range(8):
                    v.scalar_tensor_tensor(
                        dst[:, cc * 128:(cc + 1) * 128],
                        dbuf[:, cc * 128:(cc + 1) * 128],
                        mu_sb[mu][:, cc:cc + 1],
                        zf[slayer][:, cc * 129:cc * 129 + 128],
                        ALU.mult, ALU.add)
            v.drain()
            return v.tensor_copy(z3[:, :, 0:1], z3[:, :, 128:129])
        war = [track[id(dst)] for (dst, _) in targets]
        emit("v", vm, waits=_w(ev_evac, *war), inc="v")
        return ev("v")

    def proj(lhsbuf, wname, pa, pb, ev_mix, extra_w=()):
        wv = w3[wname]
        def tp(t):
            last = None
            for cc in range(8):
                st = cc == 0
                sp = cc == 7
                t.matmul(pa[:], lhsbuf[:, cc * 128:(cc + 1) * 128],
                         wv[:, cc, 0:512], start=st, stop=sp)
                last = t.matmul(pb[:], lhsbuf[:, cc * 128:(cc + 1) * 128],
                                wv[:, cc, 512:1024], start=st, stop=sp)
            return last
        waits = [ev_mix, (semDW, WIDX[wname]), EV_PRO_V] + rd_wait(pa, pb) \
            + list(extra_w)
        emit("t", tp, waits=waits, inc="t")
        e = ev("t")
        if id(lhsbuf) in track:
            track[id(lhsbuf)] = e
        return e

    def ffn(i, R, slayer, r_ev):
        evz = layer_norm(R, i, r_ev)
        evt = transpose_to(p0, z_tm[:], evz, is_z=True)
        evev = evac_z(slayer, evt)
        evm = mixes(slayer, i, [(xr_b, "murt"), (xk_b, "mukt")], evev)
        evr = proj(xr_b, "fwrt", p1, p2, evm)
        evk = proj(xk_b, "fwkt", p3, p4, evm)
        def sact(s_):
            s_.activation(krelu[:, 0:512], p3[:], AF.Relu)
            s_.activation(krelu[:, 512:1024], p4[:], AF.Relu)
            s_.activation(esig[:, 0:512], p1[:], AF.Exp, scale=-1.0)
            return s_.activation(esig[:, 512:1024], p2[:], AF.Exp, scale=-1.0)
        emit("s", sact, waits=[evk], inc="s")
        e = ev("s")
        mark_rd(p1, e); mark_rd(p2, e); mark_rd(p3, e); mark_rd(p4, e)
        def vka(v):
            v.tensor_mul(ka_tm[:], krelu[:], krelu[:])
            v.tensor_scalar(denb[:], esig[:], 1.0, None, ALU.add)
            return v.drain()
        emit("v", vka, waits=[e], inc="v")
        evka = ev("v")
        def srec(s_):
            s_.activation(lnvb2[:], denb[:], AF.Ln)
            s_.drain()
            return s_.activation(recb[:], lnvb2[:], AF.Exp, scale=-1.0)
        emit("s", srec, waits=[evka], inc="s")
        evrec = ev("s")
        evt2 = transpose_to(p0, ka_tm[:], evka)
        emit("s", lambda s_: s_.copy(ka_fm[:], p0[:]), waits=[evt2], inc="s")
        eevac = ev("s")
        mark_rd(p0, eevac)
        evq = proj(ka_fm, "fwvt", p3, p4, eevac)
        def vu(v):
            v.tensor_mul(u_sb[:, 0:512], recb[:, 0:512], p3[:])
            v.tensor_mul(u_sb[:, 512:1024], recb[:, 512:1024], p4[:])
            v.drain()
            return v.tensor_add(R[:], R[:], u_sb[:])
        wsto = [(semDO, 16 * (i - 1))] if i >= 2 else []
        emit("v", vu, waits=[evq, evrec] + wsto, inc="v")
        e = ev("v")
        mark_rd(p3, e); mark_rd(p4, e)
        return e

    def attn(i, R, r_ev):
        n = i % TPS
        evz = layer_norm(R, i, r_ev)
        evt = transpose_to(p0, z_tm[:], evz, is_z=True)
        evev = evac_z("A", evt)
        evm = mixes("A", i, [(xk_b, "mxkt"), (xv_b, "mxvt"), (xr_b, "mxrt")],
                    evev)
        evk = proj(xk_b, "wkt", p1, p2, evm)
        evv = proj(xv_b, "wvt", p3, p4, evm)
        evr = proj(xr_b, "wrt", p5, p6, evm)
        def sact(s_):
            s_.activation(ek[:, 0:512], p1[:], AF.Exp)
            s_.activation(ek[:, 512:1024], p2[:], AF.Exp)
            s_.activation(esig[:, 0:512], p5[:], AF.Exp, scale=-1.0)
            return s_.activation(esig[:, 512:1024], p6[:], AF.Exp, scale=-1.0)
        emit("s", sact, waits=[evr], inc="s")
        es_ = ev("s")
        mark_rd(p1, es_); mark_rd(p2, es_); mark_rd(p5, es_); mark_rd(p6, es_)
        def vek(v):
            if n == 0 and i > 0:
                v.memset(Aprime[:], 0.0)
            v.tensor_mul(EKVv[:, 0:8, 0:64],
                         ek[:, 0:512].rearrange("p (h d) -> p h d", h=8),
                         p3[:].rearrange("p (h d) -> p h d", h=8))
            v.tensor_mul(EKVv[:, 8:16, 0:64],
                         ek[:, 512:1024].rearrange("p (h d) -> p h d", h=8),
                         p4[:].rearrange("p (h d) -> p h d", h=8))
            v.reduce_sum(EKVv[:, :, 64], ekv16, axis=AX.X)
            v.drain()
            v.tensor_mul(ahv, EKVv, bc(utab[:], 65))
            v.tensor_scalar(denb[:], esig[:], 1.0, None, ALU.add)
            return v.drain()
        emit("v", vek, waits=[es_], inc="v")
        evek = ev("v")
        mark_rd(p3, evek); mark_rd(p4, evek)
        def sreca(s_):
            s_.activation(lnvb2[:], denb[:], AF.Ln)
            s_.drain()
            return s_.activation(recb[:], lnvb2[:], AF.Exp, scale=-1.0)
        emit("s", sreca, waits=[evek], inc="s")
        evreca = ev("s")
        PG = [p1, p2, p3, p4]
        def twkv(t):
            last = None
            for g in range(4):
                t.matmul(PG[g][:, 0:260], tri[:],
                         ahat[:, 260 * g:260 * (g + 1)],
                         start=True, stop=False)
                last = t.matmul(PG[g][:, 0:260], ones1[:],
                                Aprime[0:1, 260 * g:260 * (g + 1)],
                                start=False, stop=True)
            return last
        emit("t", twkv, waits=[evek] + rd_wait(p1, p2, p3, p4), inc="t")
        evw = ev("t")
        def vepi(v):
            for g in range(4):
                v.tensor_mul(t2v[:, 4 * g:4 * (g + 1), :],
                             PG[g][:, 0:260].rearrange("p (h c) -> p h c", h=4),
                             bc(wtab[:, 4 * g:4 * (g + 1)], 65))
            v.tensor_mul(tmpv, EKVv, bc(eu16[:], 65))
            v.tensor_mul(t2v[0:1, :, :], Apv,
                         bc(wtab[0:1, :], 65, parts=1))
            v.drain()
            v.tensor_scalar(tmpv[:, :, 64], tmpv[:, :, 64], 1e-6, None,
                            ALU.max)
            v.drain()
            v.tensor_add(numb[:], t2b[:], tmpb[:])
            for g in range(4):
                v.tensor_mul(
                    Apv[0:1, 4 * g:4 * (g + 1), :],
                    PG[g][0:1, 0:260].rearrange("p (h c) -> p h c", h=4),
                    bc(ew128[0:1, 4 * g:4 * (g + 1)], 65, parts=1))
            return v.drain()
        emit("v", vepi, waits=[evw], inc="v")
        evn = ev("v")
        for b in PG:
            mark_rd(b, evn)
        def srec16(s_):
            s_.activation(r16ln[:], numv[:, :, 64], AF.Ln)
            s_.drain()
            return s_.activation(recip16[:], r16ln[:], AF.Exp, scale=-1.0)
        emit("s", srec16, waits=[evn], inc="s")
        evr16 = ev("s")
        def vepi2(v):
            v.tensor_mul(gtv, numv[:, :, 0:64], bc(recip16[:], 64))
            v.drain()
            v.tensor_mul(g2[:], gtmp[:], recb[:])
            return v.drain()
        emit("v", vepi2, waits=[evr16, evreca], inc="v")
        evg = ev("v")
        evt2 = transpose_to(p0, g2[:], evg)
        emit("s", lambda s_: s_.copy(gfm[:], p0[:]), waits=[evt2], inc="s")
        egf = ev("s")
        mark_rd(p0, egf)
        evo = proj(gfm, "wot", p5, p6, egf)
        def vro(v):
            v.tensor_add(R[:, 0:512], R[:, 0:512], p5[:])
            return v.tensor_add(R[:, 512:1024], R[:, 512:1024], p6[:])
        emit("v", vro, waits=[evo], inc="v")
        e = ev("v")
        mark_rd(p5, e); mark_rd(p6, e)
        return e

    def store(i, r_ev):
        def stor(s_, i=i):
            return s_.dma_start(out_d[i * TKN:(i + 1) * TKN, :], Rb[i % 2][:])
        emit("s", lambda s_, i=i: stor(s_, i).then_inc(semDO, 16),
             waits=_w(r_ev))

    subl = _CFG["sublayers"]
    if subl == ("F1", "A", "F2") and n_tiles >= 2:
        rF1 = ffn(0, Rb[0], "F1", r_ev=(semDX, 16))
        for i in range(n_tiles):
            rA = attn(i, Rb[i % 2], r_ev=rF1)
            if i + 1 < n_tiles:
                rF1 = ffn(i + 1, Rb[(i + 1) % 2], "F1",
                          r_ev=(semDX, 16 * (i + 2)))
            rF2 = ffn(i, Rb[i % 2], "F2", r_ev=rA)
            store(i, rF2)
    else:
        for i in range(n_tiles):
            R = Rb[i % 2]
            r_ev = (semDX, 16 * (i + 1))
            for s in subl:
                if s == "A":
                    r_ev = attn(i, R, r_ev=r_ev)
                else:
                    r_ev = ffn(i, R, s, r_ev=r_ev)
            store(i, r_ev)

    ops["y"].append(([(semDO, 16 * n_tiles)], lambda y: y.nop(), None))

    with nc.Block() as block:
        @block.tensor
        def _(tensor):
            for ws, fn, inc in ops["t"]:
                for s_, v_ in ws:
                    tensor.wait_ge(s_, v_)
                inst = fn(tensor)
                if inc is not None and inst is not None:
                    inst.then_inc(SEMO[inc], 1)

        @block.vector
        def _(vector):
            for ws, fn, inc in ops["v"]:
                for s_, v_ in ws:
                    vector.wait_ge(s_, v_)
                inst = fn(vector)
                if inc is not None and inst is not None:
                    inst.then_inc(SEMO[inc], 1)

        @block.scalar
        def _(scalar):
            for ws, fn, inc in ops["s"]:
                for s_, v_ in ws:
                    scalar.wait_ge(s_, v_)
                inst = fn(scalar)
                if inc is not None and inst is not None:
                    inst.then_inc(SEMO[inc], 1)

        @block.sync
        def _(sync):
            for ws, fn, inc in ops["y"]:
                for s_, v_ in ws:
                    sync.wait_ge(s_, v_)
                fn(sync)

        @block.gpsimd
        def _(gpsimd):
            for ws, fn, inc in ops["g"]:
                for s_, v_ in ws:
                    gpsimd.wait_ge(s_, v_)
                fn(gpsimd)

    es.close()
    return nc


def kernel(L0_attn_x, L0_attn_mix_k, L0_attn_mix_v, L0_attn_mix_r,
           L0_attn_Wk, L0_attn_Wv, L0_attn_Wr, L0_attn_Wo,
           L0_attn_time_decay, L0_attn_time_first,
           L0_ffn_W_r, L0_ffn_W_k, L0_ffn_W_v, L0_ffn_mu_r, L0_ffn_mu_k):
    f32 = np.float32
    x = np.ascontiguousarray(np.asarray(L0_attn_x, dtype=f32))

    def wt(w):
        return np.ascontiguousarray(np.asarray(w, dtype=f32).T)

    weights = {
        "fwrt": wt(L0_ffn_W_r), "fwkt": wt(L0_ffn_W_k), "fwvt": wt(L0_ffn_W_v),
        "wkt": wt(L0_attn_Wk), "wvt": wt(L0_attn_Wv), "wrt": wt(L0_attn_Wr),
        "wot": wt(L0_attn_Wo),
    }

    def mutile(v):
        return np.ascontiguousarray(
            np.asarray(v, dtype=f32).reshape(-1).reshape(8, 128).T)

    mus = {
        "murt": mutile(L0_ffn_mu_r), "mukt": mutile(L0_ffn_mu_k),
        "mxkt": mutile(L0_attn_mix_k), "mxvt": mutile(L0_attn_mix_v),
        "mxrt": mutile(L0_attn_mix_r),
    }
    td = np.asarray(L0_attn_time_decay, dtype=f32).reshape(H)
    tf = np.asarray(L0_attn_time_first, dtype=f32).reshape(H)
    tdr = np.ascontiguousarray(np.broadcast_to(td[None, :], (128, H)))
    tfr = np.ascontiguousarray(np.broadcast_to(tf[None, :], (128, H)))

    j = np.arange(128, dtype=np.float64)
    consts = {
        "tri": (lambda t: (t.__setitem__((slice(None), 0), 1.0), t)[1])(
            np.triu(np.ones((128, 128), f32), 1)),
        "idf": np.eye(128, dtype=f32),
        "ones1": np.ones((1, 128), f32),
        "cju": (CCENT - j).astype(f32).reshape(128, 1),
        "cjw": (j - 1.0 - CCENT).astype(f32).reshape(128, 1),
        "tdr": tdr, "tfr": tfr,
    }

    nc = _build_nc()
    in_maps = []
    for c in range(NCORES):
        m = {"x": np.ascontiguousarray(
            x[2 * c:2 * c + 2].reshape(ROWS, C))}
        m.update(weights)
        m.update(mus)
        m.update(consts)
        in_maps.append(m)

    res = run_bass_kernel_spmd(nc, in_maps, core_ids=list(range(NCORES)))
    out = np.stack([np.asarray(r["out"], dtype=f32) for r in res.results])
    return out.reshape(16, 2048, 1024)


# version 04
# speedup vs baseline: 1.3580x; 1.1726x over previous
import contextlib
import numpy as np
import ml_dtypes
import concourse.bass as bass
import concourse.mybir as mybir
from concourse.bass_utils import run_bass_kernel_spmd

F32 = mybir.dt.float32
BF16 = mybir.dt.bfloat16
AF = mybir.ActivationFunctionType
ALU = mybir.AluOpType
AX = mybir.AxisListType

NCORES = 8
SEQ = 2048
C = 1024
H = 16
TKN = 128
ROWS = 4096
NT = ROWS // TKN
TPS = SEQ // TKN
CCENT = 63.5

_CFG = {"n_tiles": NT, "sublayers": ("F1", "A", "F2")}


def _build_nc():
    nc = bass.Bass("TRN2", target_bir_lowering=False)

    x_d = nc.dram_tensor("x", [ROWS, C], F32, kind="ExternalInput").ap()
    out_d = nc.dram_tensor("out", [ROWS, C], F32, kind="ExternalOutput").ap()
    WNAMES = ["fwrt", "fwkt", "fwvt", "wkt", "wvt", "wrt", "wot"]
    w_d = {n: nc.dram_tensor(n, [C, C], F32, kind="ExternalInput").ap()
           for n in WNAMES}
    tri_d = nc.dram_tensor("tri", [128, 128], F32, kind="ExternalInput").ap()
    idf_d = nc.dram_tensor("idf", [128, 128], F32, kind="ExternalInput").ap()
    ones1_d = nc.dram_tensor("ones1", [1, 128], F32, kind="ExternalInput").ap()
    cju_d = nc.dram_tensor("cju", [128, 1], F32, kind="ExternalInput").ap()
    cjw_d = nc.dram_tensor("cjw", [128, 1], F32, kind="ExternalInput").ap()
    tdr_d = nc.dram_tensor("tdr", [128, H], F32, kind="ExternalInput").ap()
    tfr_d = nc.dram_tensor("tfr", [128, H], F32, kind="ExternalInput").ap()
    mu_d = {n: nc.dram_tensor(n, [128, 8], F32, kind="ExternalInput").ap()
            for n in ["murt", "mukt", "mxkt", "mxvt", "mxrt"]}

    es = contextlib.ExitStack()
    sb = lambda name, shape, dt: es.enter_context(nc.sbuf_tensor(name, shape, dt))
    ps = lambda name, shape, dt: es.enter_context(nc.psum_tensor(name, shape, dt))
    sem = lambda name: es.enter_context(nc.semaphore(name))

    w_sb = {n: sb(n + "_s", [128, 8 * C], BF16) for n in WNAMES}
    Rb = [sb("R0", [128, C], F32), sb("R1", [128, C], F32)]
    z_tm = sb("z_tm", [128, C], BF16)
    zf = {s: sb("zf" + s, [128, 8 * 129], BF16) for s in ("F1", "A", "F2")}
    dbuf = sb("dbuf", [128, 8 * 128], BF16)
    xr_b = sb("xr", [128, 8 * 128], BF16)
    xk_b = sb("xk", [128, 8 * 128], BF16)
    xv_b = sb("xv", [128, 8 * 128], BF16)
    junk = sb("junk", [128, C], BF16)
    s1 = sb("s1", [128, 1], F32); s2 = sb("s2", [128, 1], F32)
    meanb = sb("meanb", [128, 1], F32); varb = sb("varb", [128, 1], F32)
    m2b = sb("m2b", [128, 1], F32); lnvb = sb("lnvb", [128, 1], F32)
    invb = sb("invb", [128, 1], F32); negmib = sb("negmib", [128, 1], F32)
    krelu = sb("krelu", [128, C], BF16)
    ka_tm = sb("ka_tm", [128, C], BF16)
    ka_fm = sb("ka_fm", [128, 8 * 128], BF16)
    esig = sb("esig", [128, C], F32)
    denb = sb("denb", [128, C], F32)
    recb = sb("recb", [128, C], F32)
    lnvb2 = sb("lnvb2", [128, C], F32)
    u_sb = sb("u_sb", [128, C], F32)
    ek = sb("ek", [128, C], F32)
    EKV = sb("EKV", [128, 1040], F32)
    ahat = sb("ahat", [128, 1040], F32)
    t2b = sb("t2b", [128, 1040], F32)
    tmpb = sb("tmpb", [128, 1040], F32)
    numb = sb("numb", [128, 1040], F32)
    qrow = sb("qrow", [1, 1040], F32)
    Aprime = sb("Aprime", [1, 1040], F32)
    gtmp = sb("gtmp", [128, C], F32)
    g2 = sb("g2", [128, C], BF16)
    recip16 = sb("recip16", [128, H], F32)
    r16ln = sb("r16ln", [128, H], F32)
    gfm = sb("gfm", [128, 8 * 128], BF16)
    tri = sb("tri_s", [128, 128], F32)
    idf = sb("idf_s", [128, 128], F32)
    idb = sb("idb_s", [128, 128], BF16)
    ones1 = sb("ones1_s", [1, 128], F32)
    cju = sb("cju_s", [128, 1], F32)
    cjw = sb("cjw_s", [128, 1], F32)
    tdr = sb("tdr_s", [128, H], F32)
    tfr = sb("tfr_s", [128, H], F32)
    wtil = sb("wtil", [128, H], F32)
    negw = sb("negw", [128, H], F32)
    utab = sb("utab", [128, H], F32)
    wtab = sb("wtab", [128, H], F32)
    ew128 = sb("ew128", [128, H], F32)
    eu16 = sb("eu16", [128, H], F32)
    mu_sb = {n: sb(n + "_s", [128, 8], F32) for n in mu_d}

    p0 = ps("p0", [128, 1024], BF16)
    pg = [ps(f"p{k}", [128, 512], F32) for k in range(1, 7)]
    p1, p2, p3, p4, p5, p6 = pg

    semT = sem("semT"); semV = sem("semV"); semS = sem("semS")
    semDX = sem("semDX"); semDO = sem("semDO")
    semDW = sem("semDW"); semDC = sem("semDC")

    ops = {k: [] for k in ("t", "v", "s", "y", "g")}
    cnt = {"t": 0, "v": 0, "s": 0}
    SEMO = {"t": semT, "v": semV, "s": semS}

    def emit(stream, fn, waits=(), inc=None):
        ws = [(s_, v_) for (s_, v_) in waits if v_ > 0]
        ops[stream].append((ws, fn, inc))
        if inc is not None:
            cnt[inc] += 1

    def wT(): return (semT, cnt["t"])
    def wV(): return (semV, cnt["v"])
    def wS(): return (semS, cnt["s"])

    def ev(stream):
        return (SEMO[stream], cnt[stream])

    w3 = {n: w_sb[n][:].rearrange("p (cc d) -> p cc d", cc=8) for n in WNAMES}
    zf3 = {s: zf[s][:].rearrange("p (cc t) -> p cc t", cc=8) for s in zf}
    EKVv = EKV[:].rearrange("p (h c) -> p h c", h=H)
    ahv = ahat[:].rearrange("p (h c) -> p h c", h=H)
    t2v = t2b[:].rearrange("p (h c) -> p h c", h=H)
    tmpv = tmpb[:].rearrange("p (h c) -> p h c", h=H)
    numv = numb[:].rearrange("p (h c) -> p h c", h=H)
    qrv = qrow[:].rearrange("p (h c) -> p h c", h=H)
    Apv = Aprime[:].rearrange("p (h c) -> p h c", h=H)
    ekv16 = ek[:].rearrange("p (h d) -> p h d", h=H)
    gtv = gtmp[:].rearrange("p (h d) -> p h d", h=H)

    def bc(ap2d, n65, parts=128):
        return ap2d.unsqueeze(2).broadcast_to([parts, ap2d.shape[1], n65])

    def pro_y(sync):
        for nm, dst, src in [("tri", tri, tri_d), ("idf", idf, idf_d),
                             ("ones1", ones1, ones1_d), ("cju", cju, cju_d),
                             ("cjw", cjw, cjw_d), ("tdr", tdr, tdr_d),
                             ("tfr", tfr, tfr_d)]:
            sync.dma_start(dst[:], src).then_inc(semDC, 16)
        for nm in mu_d:
            sync.dma_start(mu_sb[nm][:], mu_d[nm]).then_inc(semDC, 16)
    ops["y"].append(([], pro_y, None))
    NCONST = 12

    def pro_g(gpsimd):
        for nm in WNAMES:
            src = w_d[nm].rearrange("(cc p) d -> p cc d", p=128)
            gpsimd.dma_start(w3[nm], src).then_inc(semDW, 16)
    ops["g"].append(([], pro_g, None))
    WIDX = {n: 16 * (k + 1) for k, n in enumerate(WNAMES)}

    emit("s", lambda s_: s_.activation(wtil[:], tdr[:], AF.Exp),
         waits=[(semDC, 16 * NCONST)], inc="s")
    emit("v", lambda v: v.tensor_scalar(negw[:], wtil[:], -1.0, None, ALU.mult),
         waits=[wS()], inc="v")

    def tabs(s_):
        s_.activation(utab[:], negw[:], AF.Exp, scale=cju[:])
        s_.activation(wtab[:], negw[:], AF.Exp, scale=cjw[:])
        s_.activation(ew128[:], negw[:], AF.Exp, scale=128.0)
        return s_.activation(eu16[:], tfr[:], AF.Exp)
    emit("s", tabs, waits=[wV()], inc="s")

    def pro_v(v):
        v.tensor_copy(idb[:], idf[:])
        v.tensor_scalar(w_sb["fwvt"][:], w_sb["fwvt"][:], 0.5, None, ALU.mult)
        v.memset(Aprime[:], 0.0)
        for s in zf3:
            v.memset(zf3[s][:, :, 0:1], 0.0)
        return v.nop()
    emit("v", pro_v, waits=[(semDW, WIDX["fwvt"]), (semDC, 16 * NCONST)],
         inc="v")
    EV_PRO_V = ev("v")

    n_tiles = _CFG["n_tiles"]
    for i in range(n_tiles):
        def load(sync, i=i):
            return sync.dma_start(Rb[i % 2][:], x_d[i * TKN:(i + 1) * TKN, :])
        waits = [(semDO, 16 * (i - 1))] if i >= 2 else []
        ops["y"].append((waits, lambda y, i=i: load(y, i).then_inc(semDX, 16),
                         None))

    last_rd = {id(p0): None}
    for p in pg:
        last_rd[id(p)] = None

    def rd_wait(*banks):
        ws = []
        for b in banks:
            e = last_rd[id(b)]
            if e is not None:
                ws.append(e)
        return ws

    def mark_rd(bank, event):
        last_rd[id(bank)] = event

    track = {"v2": None, "zT": None,
             id(xr_b): None, id(xk_b): None, id(xv_b): None}

    def _w(*evs):
        return [e for e in evs if e is not None]

    def layer_norm(R, i, r_ev):
        def v1(v):
            v.reduce_sum(s1[:], R[:], axis=AX.X)
            v.drain()
            return v.tensor_scalar(meanb[:], s1[:], 1.0 / C, None, ALU.mult)
        emit("v", v1, waits=_w(r_ev), inc="v")
        emit("s", lambda s_: s_.activation(junk[:], R[:], AF.Square,
                                           accum_out=s2[:]),
             waits=_w(r_ev, track["v2"]), inc="s")
        ev_sq = ev("s")
        def v2(v):
            v.drain()
            v.tensor_scalar(varb[:], s2[:], 1.0 / C, None, ALU.mult)
            v.tensor_scalar(m2b[:], meanb[:], meanb[:], None, ALU.mult)
            v.drain()
            v.tensor_sub(varb[:], varb[:], m2b[:])
            v.drain()
            return v.tensor_scalar(varb[:], varb[:], 1e-5, None, ALU.add)
        emit("v", v2, waits=[ev_sq], inc="v")
        ev_v2 = ev("v")
        track["v2"] = ev_v2
        def s2f(s_):
            s_.activation(lnvb[:], varb[:], AF.Ln)
            s_.drain()
            return s_.activation(invb[:], lnvb[:], AF.Exp, scale=-0.5)
        emit("s", s2f, waits=[ev_v2], inc="s")
        ev_inv = ev("s")
        def v3(v):
            v.tensor_scalar(negmib[:], invb[:], meanb[:], -1.0,
                            ALU.mult, ALU.mult)
            v.drain()
            return v.tensor_scalar(z_tm[:], R[:], invb[:], negmib[:],
                                   ALU.mult, ALU.add)
        emit("v", v3, waits=_w(ev_inv, track["zT"]), inc="v")
        return ev("v")

    def transpose_to(p0bank, src, ev_src, is_z=False):
        def tf(t):
            last = None
            for cc in range(8):
                last = t.transpose(p0bank[:, cc * 128:(cc + 1) * 128],
                                   src[:, cc * 128:(cc + 1) * 128], idb[:])
            return last
        emit("t", tf, waits=[ev_src, EV_PRO_V] + rd_wait(p0bank), inc="t")
        e = ev("t")
        if is_z:
            track["zT"] = e
        return e

    def evac_z(slayer, ev_t):
        emit("s", lambda s_: s_.copy(
            zf3[slayer][:, :, 1:129],
            p0[:].rearrange("p (cc t) -> p cc t", cc=8)),
            waits=[ev_t], inc="s")
        e = ev("s")
        mark_rd(p0, e)
        return e

    def mixes(slayer, i, targets, ev_evac):
        z3 = zf3[slayer]
        n = i % TPS
        evs = []
        for ti, (dst, mu) in enumerate(targets):
            def vm(v, ti=ti, dst=dst, mu=mu):
                if ti == 0:
                    if n == 0 and i > 0:
                        v.memset(z3[:, :, 0:1], 0.0)
                    v.tensor_sub(dbuf[:].rearrange("p (cc t) -> p cc t", cc=8),
                                 z3[:, :, 1:129], z3[:, :, 0:128])
                    v.drain()
                last = None
                for cc in range(8):
                    last = v.scalar_tensor_tensor(
                        dst[:, cc * 128:(cc + 1) * 128],
                        dbuf[:, cc * 128:(cc + 1) * 128],
                        mu_sb[mu][:, cc:cc + 1],
                        zf[slayer][:, cc * 129:cc * 129 + 128],
                        ALU.mult, ALU.add)
                if ti == len(targets) - 1:
                    v.drain()
                    last = v.tensor_copy(z3[:, :, 0:1], z3[:, :, 128:129])
                return last
            war = [track[id(dst)]] if ti > 0 else \
                [track[id(d)] for (d, _) in targets]
            emit("v", vm, waits=_w(ev_evac, *war) if ti == 0 else
                 _w(*war), inc="v")
            evs.append(ev("v"))
        return evs

    def proj(lhsbuf, wname, pa, pb, ev_mix, extra_w=()):
        wv = w3[wname]
        def tp(t):
            last = None
            for cc in range(8):
                st = cc == 0
                sp = cc == 7
                t.matmul(pa[:], lhsbuf[:, cc * 128:(cc + 1) * 128],
                         wv[:, cc, 0:512], start=st, stop=sp)
                last = t.matmul(pb[:], lhsbuf[:, cc * 128:(cc + 1) * 128],
                                wv[:, cc, 512:1024], start=st, stop=sp)
            return last
        waits = [ev_mix, (semDW, WIDX[wname]), EV_PRO_V] + rd_wait(pa, pb) \
            + list(extra_w)
        emit("t", tp, waits=waits, inc="t")
        e = ev("t")
        if id(lhsbuf) in track:
            track[id(lhsbuf)] = e
        return e

    def ffn(i, R, slayer, r_ev):
        evz = layer_norm(R, i, r_ev)
        evt = transpose_to(p0, z_tm[:], evz, is_z=True)
        evev = evac_z(slayer, evt)
        evmk, evmr = mixes(slayer, i, [(xk_b, "mukt"), (xr_b, "murt")],
                           evev)
        evk = proj(xk_b, "fwkt", p3, p4, evmk)
        evr = proj(xr_b, "fwrt", p1, p2, evmr)
        def srelu(s_):
            s_.activation(krelu[:, 0:512], p3[:], AF.Relu)
            return s_.activation(krelu[:, 512:1024], p4[:], AF.Relu)
        emit("s", srelu, waits=[evk], inc="s")
        e = ev("s")
        mark_rd(p3, e); mark_rd(p4, e)
        def vka(v):
            return v.tensor_mul(ka_tm[:], krelu[:], krelu[:])
        emit("v", vka, waits=[e], inc="v")
        evka = ev("v")
        def sesig(s_):
            s_.activation(esig[:, 0:512], p1[:], AF.Exp, scale=-1.0)
            return s_.activation(esig[:, 512:1024], p2[:], AF.Exp, scale=-1.0)
        emit("s", sesig, waits=[evr], inc="s")
        e2 = ev("s")
        mark_rd(p1, e2); mark_rd(p2, e2)
        emit("v", lambda v: v.tensor_scalar(denb[:], esig[:], 1.0, None,
                                            ALU.add), waits=[e2], inc="v")
        evden = ev("v")
        def srec(s_):
            s_.activation(lnvb2[:], denb[:], AF.Ln)
            s_.drain()
            return s_.activation(recb[:], lnvb2[:], AF.Exp, scale=-1.0)
        emit("s", srec, waits=[evden], inc="s")
        evrec = ev("s")
        evt2 = transpose_to(p0, ka_tm[:], evka)
        emit("s", lambda s_: s_.copy(ka_fm[:], p0[:]), waits=[evt2], inc="s")
        eevac = ev("s")
        mark_rd(p0, eevac)
        evq = proj(ka_fm, "fwvt", p3, p4, eevac)
        def vu(v):
            v.tensor_mul(u_sb[:, 0:512], recb[:, 0:512], p3[:])
            v.tensor_mul(u_sb[:, 512:1024], recb[:, 512:1024], p4[:])
            v.drain()
            return v.tensor_add(R[:], R[:], u_sb[:])
        wsto = [(semDO, 16 * (i - 1))] if i >= 2 else []
        emit("v", vu, waits=[evq, evrec] + wsto, inc="v")
        e = ev("v")
        mark_rd(p3, e); mark_rd(p4, e)
        return e

    def attn(i, R, r_ev):
        n = i % TPS
        evz = layer_norm(R, i, r_ev)
        evt = transpose_to(p0, z_tm[:], evz, is_z=True)
        evev = evac_z("A", evt)
        evmk, evmv, evmr = mixes(
            "A", i, [(xk_b, "mxkt"), (xv_b, "mxvt"), (xr_b, "mxrt")], evev)
        evk = proj(xk_b, "wkt", p1, p2, evmk)
        evv = proj(xv_b, "wvt", p3, p4, evmv)
        def s_ek(s_):
            s_.activation(ek[:, 0:512], p1[:], AF.Exp)
            return s_.activation(ek[:, 512:1024], p2[:], AF.Exp)
        emit("s", s_ek, waits=[evk], inc="s")
        es_ = ev("s")
        mark_rd(p1, es_); mark_rd(p2, es_)
        def vek(v):
            if n == 0 and i > 0:
                v.memset(Aprime[:], 0.0)
            v.tensor_mul(EKVv[:, 0:8, 0:64],
                         ek[:, 0:512].rearrange("p (h d) -> p h d", h=8),
                         p3[:].rearrange("p (h d) -> p h d", h=8))
            v.tensor_mul(EKVv[:, 8:16, 0:64],
                         ek[:, 512:1024].rearrange("p (h d) -> p h d", h=8),
                         p4[:].rearrange("p (h d) -> p h d", h=8))
            v.reduce_sum(EKVv[:, :, 64], ekv16, axis=AX.X)
            v.drain()
            v.tensor_mul(ahv, EKVv, bc(utab[:], 65))
            return v.drain()
        emit("v", vek, waits=[es_, evv], inc="v")
        evek = ev("v")
        mark_rd(p3, evek); mark_rd(p4, evek)
        evr = proj(xr_b, "wrt", p5, p6, evmr)
        def sesig(s_):
            s_.activation(esig[:, 0:512], p5[:], AF.Exp, scale=-1.0)
            return s_.activation(esig[:, 512:1024], p6[:], AF.Exp, scale=-1.0)
        emit("s", sesig, waits=[evr], inc="s")
        e2 = ev("s")
        mark_rd(p5, e2); mark_rd(p6, e2)
        emit("v", lambda v: v.tensor_scalar(denb[:], esig[:], 1.0, None,
                                            ALU.add), waits=[e2], inc="v")
        evden = ev("v")
        def sreca(s_):
            s_.activation(lnvb2[:], denb[:], AF.Ln)
            s_.drain()
            return s_.activation(recb[:], lnvb2[:], AF.Exp, scale=-1.0)
        emit("s", sreca, waits=[evden], inc="s")
        evreca = ev("s")
        PG = [p1, p2, p3, p4]
        def twkv(t):
            last = None
            for g in range(4):
                t.matmul(PG[g][:, 0:260], tri[:],
                         ahat[:, 260 * g:260 * (g + 1)],
                         start=True, stop=False)
                last = t.matmul(PG[g][:, 0:260], ones1[:],
                                Aprime[0:1, 260 * g:260 * (g + 1)],
                                start=False, stop=True)
            return last
        emit("t", twkv, waits=[evek] + rd_wait(p1, p2, p3, p4), inc="t")
        evw = ev("t")
        def vepi(v):
            for g in range(4):
                v.tensor_mul(t2v[:, 4 * g:4 * (g + 1), :],
                             PG[g][:, 0:260].rearrange("p (h c) -> p h c", h=4),
                             bc(wtab[:, 4 * g:4 * (g + 1)], 65))
            v.tensor_mul(tmpv, EKVv, bc(eu16[:], 65))
            v.tensor_mul(t2v[0:1, :, :], Apv,
                         bc(wtab[0:1, :], 65, parts=1))
            v.drain()
            v.tensor_scalar(tmpv[:, :, 64], tmpv[:, :, 64], 1e-6, None,
                            ALU.max)
            v.drain()
            v.tensor_add(numb[:], t2b[:], tmpb[:])
            for g in range(4):
                v.tensor_mul(
                    Apv[0:1, 4 * g:4 * (g + 1), :],
                    PG[g][0:1, 0:260].rearrange("p (h c) -> p h c", h=4),
                    bc(ew128[0:1, 4 * g:4 * (g + 1)], 65, parts=1))
            return v.drain()
        emit("v", vepi, waits=[evw], inc="v")
        evn = ev("v")
        for b in PG:
            mark_rd(b, evn)
        def srec16(s_):
            s_.activation(r16ln[:], numv[:, :, 64], AF.Ln)
            s_.drain()
            return s_.activation(recip16[:], r16ln[:], AF.Exp, scale=-1.0)
        emit("s", srec16, waits=[evn], inc="s")
        evr16 = ev("s")
        def vepi2(v):
            v.tensor_mul(gtv, numv[:, :, 0:64], bc(recip16[:], 64))
            v.drain()
            v.tensor_mul(g2[:], gtmp[:], recb[:])
            return v.drain()
        emit("v", vepi2, waits=[evr16, evreca], inc="v")
        evg = ev("v")
        evt2 = transpose_to(p0, g2[:], evg)
        emit("s", lambda s_: s_.copy(gfm[:], p0[:]), waits=[evt2], inc="s")
        egf = ev("s")
        mark_rd(p0, egf)
        evo = proj(gfm, "wot", p5, p6, egf)
        def vro(v):
            v.tensor_add(R[:, 0:512], R[:, 0:512], p5[:])
            return v.tensor_add(R[:, 512:1024], R[:, 512:1024], p6[:])
        emit("v", vro, waits=[evo], inc="v")
        e = ev("v")
        mark_rd(p5, e); mark_rd(p6, e)
        return e

    def store(i, r_ev):
        def stor(s_, i=i):
            return s_.dma_start(out_d[i * TKN:(i + 1) * TKN, :], Rb[i % 2][:])
        emit("s", lambda s_, i=i: stor(s_, i).then_inc(semDO, 16),
             waits=_w(r_ev))

    subl = _CFG["sublayers"]
    if subl == ("F1", "A", "F2") and n_tiles >= 2:
        rF1 = ffn(0, Rb[0], "F1", r_ev=(semDX, 16))
        for i in range(n_tiles):
            rA = attn(i, Rb[i % 2], r_ev=rF1)
            if i + 1 < n_tiles:
                rF1 = ffn(i + 1, Rb[(i + 1) % 2], "F1",
                          r_ev=(semDX, 16 * (i + 2)))
            rF2 = ffn(i, Rb[i % 2], "F2", r_ev=rA)
            store(i, rF2)
    else:
        for i in range(n_tiles):
            R = Rb[i % 2]
            r_ev = (semDX, 16 * (i + 1))
            for s in subl:
                if s == "A":
                    r_ev = attn(i, R, r_ev=r_ev)
                else:
                    r_ev = ffn(i, R, s, r_ev=r_ev)
            store(i, r_ev)

    ops["y"].append(([(semDO, 16 * n_tiles)], lambda y: y.nop(), None))

    with nc.Block() as block:
        @block.tensor
        def _(tensor):
            for ws, fn, inc in ops["t"]:
                for s_, v_ in ws:
                    tensor.wait_ge(s_, v_)
                inst = fn(tensor)
                if inc is not None and inst is not None:
                    inst.then_inc(SEMO[inc], 1)

        @block.vector
        def _(vector):
            for ws, fn, inc in ops["v"]:
                for s_, v_ in ws:
                    vector.wait_ge(s_, v_)
                inst = fn(vector)
                if inc is not None and inst is not None:
                    inst.then_inc(SEMO[inc], 1)

        @block.scalar
        def _(scalar):
            for ws, fn, inc in ops["s"]:
                for s_, v_ in ws:
                    scalar.wait_ge(s_, v_)
                inst = fn(scalar)
                if inc is not None and inst is not None:
                    inst.then_inc(SEMO[inc], 1)

        @block.sync
        def _(sync):
            for ws, fn, inc in ops["y"]:
                for s_, v_ in ws:
                    sync.wait_ge(s_, v_)
                fn(sync)

        @block.gpsimd
        def _(gpsimd):
            for ws, fn, inc in ops["g"]:
                for s_, v_ in ws:
                    gpsimd.wait_ge(s_, v_)
                fn(gpsimd)

    es.close()
    return nc


def kernel(L0_attn_x, L0_attn_mix_k, L0_attn_mix_v, L0_attn_mix_r,
           L0_attn_Wk, L0_attn_Wv, L0_attn_Wr, L0_attn_Wo,
           L0_attn_time_decay, L0_attn_time_first,
           L0_ffn_W_r, L0_ffn_W_k, L0_ffn_W_v, L0_ffn_mu_r, L0_ffn_mu_k):
    f32 = np.float32
    x = np.ascontiguousarray(np.asarray(L0_attn_x, dtype=f32))

    def wt(w):
        return np.ascontiguousarray(np.asarray(w, dtype=f32).T)

    weights = {
        "fwrt": wt(L0_ffn_W_r), "fwkt": wt(L0_ffn_W_k), "fwvt": wt(L0_ffn_W_v),
        "wkt": wt(L0_attn_Wk), "wvt": wt(L0_attn_Wv), "wrt": wt(L0_attn_Wr),
        "wot": wt(L0_attn_Wo),
    }

    def mutile(v):
        return np.ascontiguousarray(
            np.asarray(v, dtype=f32).reshape(-1).reshape(8, 128).T)

    mus = {
        "murt": mutile(L0_ffn_mu_r), "mukt": mutile(L0_ffn_mu_k),
        "mxkt": mutile(L0_attn_mix_k), "mxvt": mutile(L0_attn_mix_v),
        "mxrt": mutile(L0_attn_mix_r),
    }
    td = np.asarray(L0_attn_time_decay, dtype=f32).reshape(H)
    tf = np.asarray(L0_attn_time_first, dtype=f32).reshape(H)
    tdr = np.ascontiguousarray(np.broadcast_to(td[None, :], (128, H)))
    tfr = np.ascontiguousarray(np.broadcast_to(tf[None, :], (128, H)))

    j = np.arange(128, dtype=np.float64)
    consts = {
        "tri": (lambda t: (t.__setitem__((slice(None), 0), 1.0), t)[1])(
            np.triu(np.ones((128, 128), f32), 1)),
        "idf": np.eye(128, dtype=f32),
        "ones1": np.ones((1, 128), f32),
        "cju": (CCENT - j).astype(f32).reshape(128, 1),
        "cjw": (j - 1.0 - CCENT).astype(f32).reshape(128, 1),
        "tdr": tdr, "tfr": tfr,
    }

    nc = _build_nc()
    in_maps = []
    for c in range(NCORES):
        m = {"x": np.ascontiguousarray(
            x[2 * c:2 * c + 2].reshape(ROWS, C))}
        m.update(weights)
        m.update(mus)
        m.update(consts)
        in_maps.append(m)

    res = run_bass_kernel_spmd(nc, in_maps, core_ids=list(range(NCORES)))
    out = np.stack([np.asarray(r["out"], dtype=f32) for r in res.results])
    return out.reshape(16, 2048, 1024)


# version 05
# speedup vs baseline: 1.4666x; 1.0800x over previous
import contextlib
import numpy as np
import ml_dtypes
import concourse.bass as bass
import concourse.mybir as mybir
from concourse.bass_utils import run_bass_kernel_spmd

F32 = mybir.dt.float32
BF16 = mybir.dt.bfloat16
AF = mybir.ActivationFunctionType
ALU = mybir.AluOpType
AX = mybir.AxisListType

NCORES = 8
SEQ = 2048
C = 1024
H = 16
TKN = 128
ROWS = 4096
NT = ROWS // TKN
TPS = SEQ // TKN
CCENT = 63.5

_CFG = {"n_tiles": NT, "sublayers": ("F1", "A", "F2")}


def _build_nc():
    nc = bass.Bass("TRN2", target_bir_lowering=False)

    x_d = nc.dram_tensor("x", [ROWS, C], F32, kind="ExternalInput").ap()
    out_d = nc.dram_tensor("out", [ROWS, C], F32, kind="ExternalOutput").ap()
    WNAMES = ["fwrt", "fwkt", "fwvt", "wkt", "wvt", "wrt", "wot"]
    w_d = {n: nc.dram_tensor(n, [C, C], F32, kind="ExternalInput").ap()
           for n in WNAMES}
    tri_d = nc.dram_tensor("tri", [128, 128], F32, kind="ExternalInput").ap()
    idf_d = nc.dram_tensor("idf", [128, 128], F32, kind="ExternalInput").ap()
    ones1_d = nc.dram_tensor("ones1", [1, 128], F32, kind="ExternalInput").ap()
    cju_d = nc.dram_tensor("cju", [128, 1], F32, kind="ExternalInput").ap()
    cjw_d = nc.dram_tensor("cjw", [128, 1], F32, kind="ExternalInput").ap()
    tdr_d = nc.dram_tensor("tdr", [128, H], F32, kind="ExternalInput").ap()
    tfr_d = nc.dram_tensor("tfr", [128, H], F32, kind="ExternalInput").ap()
    mu_d = {n: nc.dram_tensor(n, [128, 8], F32, kind="ExternalInput").ap()
            for n in ["murt", "mukt", "mxkt", "mxvt", "mxrt"]}

    es = contextlib.ExitStack()
    sb = lambda name, shape, dt: es.enter_context(nc.sbuf_tensor(name, shape, dt))
    ps = lambda name, shape, dt: es.enter_context(nc.psum_tensor(name, shape, dt))
    sem = lambda name: es.enter_context(nc.semaphore(name))

    w_sb = {n: sb(n + "_s", [128, 8 * C], BF16) for n in WNAMES}
    Rb = [sb("R0", [128, C], F32), sb("R1", [128, C], F32),
          sb("R2", [128, C], F32)]
    z_tm = sb("z_tm", [128, C], BF16)
    zf = {s: sb("zf" + s, [128, 8 * 129], BF16) for s in ("F1", "A", "F2")}
    dbuf = sb("dbuf", [128, 8 * 128], BF16)
    xr_b = sb("xr", [128, 8 * 128], BF16)
    xk_b = sb("xk", [128, 8 * 128], BF16)
    xv_b = sb("xv", [128, 8 * 128], BF16)
    junk = sb("junk", [128, C], BF16)
    s1 = sb("s1", [128, 1], F32); s2 = sb("s2", [128, 1], F32)
    meanb = sb("meanb", [128, 1], F32); varb = sb("varb", [128, 1], F32)
    m2b = sb("m2b", [128, 1], F32); lnvb = sb("lnvb", [128, 1], F32)
    invb = sb("invb", [128, 1], F32); negmib = sb("negmib", [128, 1], F32)
    krelu = sb("krelu", [128, C], BF16)
    ka_tm = sb("ka_tm", [128, C], BF16)
    ka_fm = sb("ka_fm", [128, 8 * 128], BF16)
    esig = sb("esig", [128, C], F32)
    denb = sb("denb", [128, C], F32)
    recbA = sb("recbA", [128, C], F32)
    recbF = sb("recbF", [128, C], F32)
    u_sb = sb("u_sb", [128, C], F32)
    ek = sb("ek", [128, C], F32)
    EKV = sb("EKV", [128, 1040], F32)
    ahat = sb("ahat", [128, 1040], F32)
    t2b = sb("t2b", [128, 1040], F32)
    tmpb = sb("tmpb", [128, 1040], F32)
    numb = sb("numb", [128, 1040], F32)
    qrow = sb("qrow", [1, 1040], F32)
    Aprime = sb("Aprime", [1, 1040], F32)
    g2 = sb("g2", [128, C], BF16)
    recip16 = sb("recip16", [128, H], F32)
    r16ln = sb("r16ln", [128, H], F32)
    gfm = sb("gfm", [128, 8 * 128], BF16)
    tri = sb("tri_s", [128, 128], F32)
    idf = sb("idf_s", [128, 128], F32)
    idb = sb("idb_s", [128, 128], BF16)
    ones1 = sb("ones1_s", [1, 128], F32)
    cju = sb("cju_s", [128, 1], F32)
    cjw = sb("cjw_s", [128, 1], F32)
    tdr = sb("tdr_s", [128, H], F32)
    tfr = sb("tfr_s", [128, H], F32)
    wtil = sb("wtil", [128, H], F32)
    negw = sb("negw", [128, H], F32)
    utab = sb("utab", [128, H], F32)
    wtab = sb("wtab", [128, H], F32)
    ew128 = sb("ew128", [128, H], F32)
    eu16 = sb("eu16", [128, H], F32)
    mu_sb = {n: sb(n + "_s", [128, 8], F32) for n in mu_d}

    p0 = ps("p0", [128, 1024], BF16)
    pg = [ps(f"p{k}", [128, 512], F32) for k in range(1, 7)]
    p1, p2, p3, p4, p5, p6 = pg

    semT = sem("semT"); semV = sem("semV"); semS = sem("semS")
    semDX = sem("semDX"); semDO = sem("semDO")
    semDW = sem("semDW"); semDC = sem("semDC")

    ops = {k: [] for k in ("t", "v", "s", "y", "g")}
    cnt = {"t": 0, "v": 0, "s": 0}
    SEMO = {"t": semT, "v": semV, "s": semS}

    def emit(stream, fn, waits=(), inc=None):
        ws = [(s_, v_) for (s_, v_) in waits if v_ > 0]
        ops[stream].append((ws, fn, inc))
        if inc is not None:
            cnt[inc] += 1

    def wT(): return (semT, cnt["t"])
    def wV(): return (semV, cnt["v"])
    def wS(): return (semS, cnt["s"])

    def ev(stream):
        return (SEMO[stream], cnt[stream])

    w3 = {n: w_sb[n][:].rearrange("p (cc d) -> p cc d", cc=8) for n in WNAMES}
    zf3 = {s: zf[s][:].rearrange("p (cc t) -> p cc t", cc=8) for s in zf}
    EKVv = EKV[:].rearrange("p (h c) -> p h c", h=H)
    ahv = ahat[:].rearrange("p (h c) -> p h c", h=H)
    t2v = t2b[:].rearrange("p (h c) -> p h c", h=H)
    tmpv = tmpb[:].rearrange("p (h c) -> p h c", h=H)
    numv = numb[:].rearrange("p (h c) -> p h c", h=H)
    qrv = qrow[:].rearrange("p (h c) -> p h c", h=H)
    Apv = Aprime[:].rearrange("p (h c) -> p h c", h=H)
    ekv16 = ek[:].rearrange("p (h d) -> p h d", h=H)

    def bc(ap2d, n65, parts=128):
        return ap2d.unsqueeze(2).broadcast_to([parts, ap2d.shape[1], n65])

    def pro_y(sync):
        for nm, dst, src in [("tri", tri, tri_d), ("idf", idf, idf_d),
                             ("ones1", ones1, ones1_d), ("cju", cju, cju_d),
                             ("cjw", cjw, cjw_d), ("tdr", tdr, tdr_d),
                             ("tfr", tfr, tfr_d)]:
            sync.dma_start(dst[:], src).then_inc(semDC, 16)
        for nm in mu_d:
            sync.dma_start(mu_sb[nm][:], mu_d[nm]).then_inc(semDC, 16)
    ops["y"].append(([], pro_y, None))
    NCONST = 12

    def pro_g(gpsimd):
        for nm in WNAMES:
            src = w_d[nm].rearrange("(cc p) d -> p cc d", p=128)
            gpsimd.dma_start(w3[nm], src).then_inc(semDW, 16)
    ops["g"].append(([], pro_g, None))
    WIDX = {n: 16 * (k + 1) for k, n in enumerate(WNAMES)}

    emit("s", lambda s_: s_.activation(wtil[:], tdr[:], AF.Exp),
         waits=[(semDC, 16 * NCONST)], inc="s")
    emit("v", lambda v: v.tensor_scalar(negw[:], wtil[:], -1.0, None, ALU.mult),
         waits=[wS()], inc="v")

    def tabs(s_):
        s_.activation(utab[:], negw[:], AF.Exp, scale=cju[:])
        s_.activation(wtab[:], negw[:], AF.Exp, scale=cjw[:])
        s_.activation(ew128[:], negw[:], AF.Exp, scale=128.0)
        return s_.activation(eu16[:], tfr[:], AF.Exp)
    emit("s", tabs, waits=[wV()], inc="s")

    def pro_v(v):
        v.tensor_copy(idb[:], idf[:])
        v.tensor_scalar(w_sb["fwvt"][:], w_sb["fwvt"][:], 0.5, None, ALU.mult)
        v.memset(Aprime[:], 0.0)
        for s in zf3:
            v.memset(zf3[s][:, :, 0:1], 0.0)
        return v.nop()
    emit("v", pro_v, waits=[(semDW, WIDX["fwvt"]), (semDC, 16 * NCONST)],
         inc="v")
    EV_PRO_V = ev("v")

    n_tiles = _CFG["n_tiles"]
    for i in range(n_tiles):
        def load(sync, i=i):
            return sync.dma_start(Rb[i % 3][:], x_d[i * TKN:(i + 1) * TKN, :])
        waits = [(semDO, 16 * (i - 2))] if i >= 3 else []
        ops["y"].append((waits, lambda y, i=i: load(y, i).then_inc(semDX, 16),
                         None))

    last_rd = {id(p0): None}
    for p in pg:
        last_rd[id(p)] = None

    def rd_wait(*banks):
        ws = []
        for b in banks:
            e = last_rd[id(b)]
            if e is not None:
                ws.append(e)
        return ws

    def mark_rd(bank, event):
        last_rd[id(bank)] = event

    track = {"v2": None, "zT": None,
             id(xr_b): None, id(xk_b): None, id(xv_b): None}

    def _w(*evs):
        return [e for e in evs if e is not None]

    def layer_norm(R, i, r_ev):
        def v1(v):
            v.reduce_sum(s1[:], R[:], axis=AX.X)
            v.drain()
            return v.tensor_scalar(meanb[:], s1[:], 1.0 / C, None, ALU.mult)
        emit("v", v1, waits=_w(r_ev), inc="v")
        emit("s", lambda s_: s_.activation(junk[:], R[:], AF.Square,
                                           accum_out=s2[:]),
             waits=_w(r_ev, track["v2"]), inc="s")
        ev_sq = ev("s")
        def v2(v):
            v.drain()
            v.tensor_scalar(varb[:], s2[:], 1.0 / C, None, ALU.mult)
            v.tensor_scalar(m2b[:], meanb[:], meanb[:], None, ALU.mult)
            v.drain()
            v.tensor_sub(varb[:], varb[:], m2b[:])
            v.drain()
            return v.tensor_scalar(varb[:], varb[:], 1e-5, None, ALU.add)
        emit("v", v2, waits=[ev_sq], inc="v")
        ev_v2 = ev("v")
        track["v2"] = ev_v2
        def s2f(s_):
            s_.activation(lnvb[:], varb[:], AF.Ln)
            s_.drain()
            return s_.activation(invb[:], lnvb[:], AF.Exp, scale=-0.5)
        emit("s", s2f, waits=[ev_v2], inc="s")
        ev_inv = ev("s")
        def v3(v):
            v.tensor_scalar(negmib[:], invb[:], meanb[:], -1.0,
                            ALU.mult, ALU.mult)
            v.drain()
            return v.tensor_scalar(z_tm[:], R[:], invb[:], negmib[:],
                                   ALU.mult, ALU.add)
        emit("v", v3, waits=_w(ev_inv, track["zT"]), inc="v")
        return ev("v")

    def transpose_to(p0bank, src, ev_src, is_z=False):
        def tf(t):
            last = None
            for cc in range(8):
                last = t.transpose(p0bank[:, cc * 128:(cc + 1) * 128],
                                   src[:, cc * 128:(cc + 1) * 128], idb[:])
            return last
        emit("t", tf, waits=[ev_src, EV_PRO_V] + rd_wait(p0bank), inc="t")
        e = ev("t")
        if is_z:
            track["zT"] = e
        return e

    def evac_z(slayer, ev_t):
        emit("s", lambda s_: s_.copy(
            zf3[slayer][:, :, 1:129],
            p0[:].rearrange("p (cc t) -> p cc t", cc=8)),
            waits=[ev_t], inc="s")
        e = ev("s")
        mark_rd(p0, e)
        return e

    def mixes(slayer, i, targets, ev_evac):
        z3 = zf3[slayer]
        n = i % TPS
        evs = []
        for ti, (dst, mu) in enumerate(targets):
            def vm(v, ti=ti, dst=dst, mu=mu):
                if ti == 0:
                    if n == 0 and i > 0:
                        v.memset(z3[:, :, 0:1], 0.0)
                    v.tensor_sub(dbuf[:].rearrange("p (cc t) -> p cc t", cc=8),
                                 z3[:, :, 1:129], z3[:, :, 0:128])
                    v.drain()
                last = None
                for cc in range(8):
                    last = v.scalar_tensor_tensor(
                        dst[:, cc * 128:(cc + 1) * 128],
                        dbuf[:, cc * 128:(cc + 1) * 128],
                        mu_sb[mu][:, cc:cc + 1],
                        zf[slayer][:, cc * 129:cc * 129 + 128],
                        ALU.mult, ALU.add)
                if ti == len(targets) - 1:
                    v.drain()
                    last = v.tensor_copy(z3[:, :, 0:1], z3[:, :, 128:129])
                return last
            war = [track[id(dst)]] if ti > 0 else \
                [track[id(d)] for (d, _) in targets]
            emit("v", vm, waits=_w(ev_evac, *war) if ti == 0 else
                 _w(*war), inc="v")
            evs.append(ev("v"))
        return evs

    def proj(lhsbuf, wname, pa, pb, ev_mix, extra_w=()):
        wv = w3[wname]
        def tp(t):
            last = None
            for cc in range(8):
                st = cc == 0
                sp = cc == 7
                t.matmul(pa[:], lhsbuf[:, cc * 128:(cc + 1) * 128],
                         wv[:, cc, 0:512], start=st, stop=sp)
                last = t.matmul(pb[:], lhsbuf[:, cc * 128:(cc + 1) * 128],
                                wv[:, cc, 512:1024], start=st, stop=sp)
            return last
        waits = [ev_mix, (semDW, WIDX[wname]), EV_PRO_V] + rd_wait(pa, pb) \
            + list(extra_w)
        emit("t", tp, waits=waits, inc="t")
        e = ev("t")
        if id(lhsbuf) in track:
            track[id(lhsbuf)] = e
        return e

    def ffn_a(i, R, slayer, r_ev):
        evz = layer_norm(R, i, r_ev)
        evt = transpose_to(p0, z_tm[:], evz, is_z=True)
        evev = evac_z(slayer, evt)
        evmr, evmk = mixes(slayer, i, [(xr_b, "murt"), (xk_b, "mukt")], evev)
        evr = proj(xr_b, "fwrt", p5, p6, evmr)
        def sesig(s_):
            s_.activation(esig[:, 0:512], p5[:], AF.Exp, scale=-1.0)
            return s_.activation(esig[:, 512:1024], p6[:], AF.Exp, scale=-1.0)
        emit("s", sesig, waits=[evr], inc="s")
        e2 = ev("s")
        mark_rd(p5, e2); mark_rd(p6, e2)
        emit("v", lambda v: v.tensor_scalar(denb[:], esig[:], 1.0, None,
                                            ALU.add), waits=[e2], inc="v")
        evden = ev("v")
        def srec(s_):
            s_.activation(denb[:], denb[:], AF.Ln)
            s_.drain()
            return s_.activation(recbF[:], denb[:], AF.Exp, scale=-1.0)
        emit("s", srec, waits=[evden], inc="s")
        evrec = ev("s")
        evk = proj(xk_b, "fwkt", p5, p6, evmk)
        def srelu(s_):
            s_.activation(krelu[:, 0:512], p5[:], AF.Relu)
            return s_.activation(krelu[:, 512:1024], p6[:], AF.Relu)
        emit("s", srelu, waits=[evk], inc="s")
        e = ev("s")
        mark_rd(p5, e); mark_rd(p6, e)
        def vka(v):
            return v.tensor_mul(ka_tm[:], krelu[:], krelu[:])
        emit("v", vka, waits=[e], inc="v")
        return ev("v"), evrec

    def ffn_b(i, R, evka, evrec):
        evt2 = transpose_to(p0, ka_tm[:], evka)
        emit("s", lambda s_: s_.copy(ka_fm[:], p0[:]), waits=[evt2], inc="s")
        eevac = ev("s")
        mark_rd(p0, eevac)
        evq = proj(ka_fm, "fwvt", p5, p6, eevac)
        def vu(v):
            v.tensor_mul(u_sb[:, 0:512], recbF[:, 0:512], p5[:])
            v.tensor_mul(u_sb[:, 512:1024], recbF[:, 512:1024], p6[:])
            v.drain()
            return v.tensor_add(R[:], R[:], u_sb[:])
        emit("v", vu, waits=[evq, evrec], inc="v")
        e = ev("v")
        mark_rd(p5, e); mark_rd(p6, e)
        return e

    def ffn(i, R, slayer, r_ev):
        evka, evrec = ffn_a(i, R, slayer, r_ev)
        return ffn_b(i, R, evka, evrec)

    def attn_a(i, R, r_ev):
        n = i % TPS
        evz = layer_norm(R, i, r_ev)
        evt = transpose_to(p0, z_tm[:], evz, is_z=True)
        evev = evac_z("A", evt)
        evmk, evmv, evmr = mixes(
            "A", i, [(xk_b, "mxkt"), (xv_b, "mxvt"), (xr_b, "mxrt")], evev)
        evk = proj(xk_b, "wkt", p1, p2, evmk)
        evv = proj(xv_b, "wvt", p3, p4, evmv)
        def s_ek(s_):
            s_.activation(ek[:, 0:512], p1[:], AF.Exp)
            return s_.activation(ek[:, 512:1024], p2[:], AF.Exp)
        emit("s", s_ek, waits=[evk], inc="s")
        es_ = ev("s")
        mark_rd(p1, es_); mark_rd(p2, es_)
        def vek(v):
            if n == 0 and i > 0:
                v.memset(Aprime[:], 0.0)
            v.tensor_mul(EKVv[:, 0:8, 0:64],
                         ek[:, 0:512].rearrange("p (h d) -> p h d", h=8),
                         p3[:].rearrange("p (h d) -> p h d", h=8))
            v.tensor_mul(EKVv[:, 8:16, 0:64],
                         ek[:, 512:1024].rearrange("p (h d) -> p h d", h=8),
                         p4[:].rearrange("p (h d) -> p h d", h=8))
            v.reduce_sum(EKVv[:, :, 64], ekv16, axis=AX.X)
            v.drain()
            v.tensor_mul(ahv, EKVv, bc(utab[:], 65))
            return v.drain()
        emit("v", vek, waits=[es_, evv], inc="v")
        evek = ev("v")
        mark_rd(p3, evek); mark_rd(p4, evek)
        evr = proj(xr_b, "wrt", p1, p2, evmr)
        def sesig(s_):
            s_.activation(esig[:, 0:512], p1[:], AF.Exp, scale=-1.0)
            return s_.activation(esig[:, 512:1024], p2[:], AF.Exp, scale=-1.0)
        emit("s", sesig, waits=[evr], inc="s")
        e2 = ev("s")
        mark_rd(p1, e2); mark_rd(p2, e2)
        emit("v", lambda v: v.tensor_scalar(denb[:], esig[:], 1.0, None,
                                            ALU.add), waits=[e2], inc="v")
        evden = ev("v")
        def sreca(s_):
            s_.activation(denb[:], denb[:], AF.Ln)
            s_.drain()
            return s_.activation(recbA[:], denb[:], AF.Exp, scale=-1.0)
        emit("s", sreca, waits=[evden], inc="s")
        return evek, ev("s")

    def attn_b(i, evek, evreca):
        PG = [p1, p2, p3, p4]
        def twkv(t):
            last = None
            for g in range(4):
                t.matmul(PG[g][:, 0:260], tri[:],
                         ahat[:, 260 * g:260 * (g + 1)],
                         start=True, stop=False)
                last = t.matmul(PG[g][:, 0:260], ones1[:],
                                Aprime[0:1, 260 * g:260 * (g + 1)],
                                start=False, stop=True)
            return last
        emit("t", twkv, waits=[evek] + rd_wait(p1, p2, p3, p4), inc="t")
        evw = ev("t")
        def vepi(v):
            for g in range(4):
                v.tensor_mul(t2v[:, 4 * g:4 * (g + 1), :],
                             PG[g][:, 0:260].rearrange("p (h c) -> p h c", h=4),
                             bc(wtab[:, 4 * g:4 * (g + 1)], 65))
            v.tensor_mul(tmpv, EKVv, bc(eu16[:], 65))
            v.tensor_mul(t2v[0:1, :, :], Apv,
                         bc(wtab[0:1, :], 65, parts=1))
            v.drain()
            v.tensor_scalar(tmpv[:, :, 64], tmpv[:, :, 64], 1e-6, None,
                            ALU.max)
            v.drain()
            v.tensor_add(numb[:], t2b[:], tmpb[:])
            for g in range(4):
                v.tensor_mul(
                    Apv[0:1, 4 * g:4 * (g + 1), :],
                    PG[g][0:1, 0:260].rearrange("p (h c) -> p h c", h=4),
                    bc(ew128[0:1, 4 * g:4 * (g + 1)], 65, parts=1))
            return v.drain()
        emit("v", vepi, waits=[evw], inc="v")
        evn = ev("v")
        for b in PG:
            mark_rd(b, evn)
        def srec16(s_):
            s_.activation(r16ln[:], numv[:, :, 64], AF.Ln)
            s_.drain()
            return s_.activation(recip16[:], r16ln[:], AF.Exp, scale=-1.0)
        emit("s", srec16, waits=[evn], inc="s")
        evr16 = ev("s")
        def vepi2(v):
            v.tensor_mul(t2v[:, :, 0:64], numv[:, :, 0:64],
                         bc(recip16[:], 64))
            v.drain()
            v.tensor_mul(g2[:].rearrange("p (h d) -> p h d", h=H),
                         t2v[:, :, 0:64], recbA[:].rearrange(
                             "p (h d) -> p h d", h=H))
            return v.drain()
        emit("v", vepi2, waits=[evr16, evreca], inc="v")
        return ev("v")

    def attn_d(i, R, evg):
        evt2 = transpose_to(p0, g2[:], evg)
        emit("s", lambda s_: s_.copy(gfm[:], p0[:]), waits=[evt2], inc="s")
        egf = ev("s")
        mark_rd(p0, egf)
        evo = proj(gfm, "wot", p1, p2, egf)
        def vro(v):
            v.tensor_add(R[:, 0:512], R[:, 0:512], p1[:])
            return v.tensor_add(R[:, 512:1024], R[:, 512:1024], p2[:])
        emit("v", vro, waits=[evo], inc="v")
        e = ev("v")
        mark_rd(p1, e); mark_rd(p2, e)
        return e

    def attn(i, R, r_ev):
        evek, evreca = attn_a(i, R, r_ev)
        evg = attn_b(i, evek, evreca)
        return attn_d(i, R, evg)

    def store(i, r_ev):
        def stor(s_, i=i):
            return s_.dma_start(out_d[i * TKN:(i + 1) * TKN, :], Rb[i % 3][:])
        emit("s", lambda s_, i=i: stor(s_, i).then_inc(semDO, 16),
             waits=_w(r_ev))

    subl = _CFG["sublayers"]
    if subl == ("F1", "A", "F2") and n_tiles >= 3:
        st_f1 = {}
        st_a = {}
        rF1 = {}
        st_f1[0] = ffn_a(0, Rb[0], "F1", r_ev=(semDX, 16))
        rF1[0] = ffn_b(0, Rb[0], *st_f1[0])
        st_a[0] = attn_a(0, Rb[0], r_ev=rF1[0])
        st_f1[1] = ffn_a(1, Rb[1], "F1", r_ev=(semDX, 32))
        for i in range(n_tiles):
            evg = attn_b(i, *st_a[i])
            if i + 1 < n_tiles:
                rF1[i + 1] = ffn_b(i + 1, Rb[(i + 1) % 3], *st_f1[i + 1])
            rA = attn_d(i, Rb[i % 3], evg)
            st_f2 = ffn_a(i, Rb[i % 3], "F2", r_ev=rA)
            if i + 1 < n_tiles:
                st_a[i + 1] = attn_a(i + 1, Rb[(i + 1) % 3], r_ev=rF1[i + 1])
            rF2 = ffn_b(i, Rb[i % 3], *st_f2)
            store(i, rF2)
            if i + 2 < n_tiles:
                st_f1[i + 2] = ffn_a(i + 2, Rb[(i + 2) % 3], "F1",
                                     r_ev=(semDX, 16 * (i + 3)))
    else:
        for i in range(n_tiles):
            R = Rb[i % 3]
            r_ev = (semDX, 16 * (i + 1))
            for s in subl:
                if s == "A":
                    r_ev = attn(i, R, r_ev=r_ev)
                else:
                    r_ev = ffn(i, R, s, r_ev=r_ev)
            store(i, r_ev)

    ops["y"].append(([(semDO, 16 * n_tiles)], lambda y: y.nop(), None))

    with nc.Block() as block:
        @block.tensor
        def _(tensor):
            for ws, fn, inc in ops["t"]:
                for s_, v_ in ws:
                    tensor.wait_ge(s_, v_)
                inst = fn(tensor)
                if inc is not None and inst is not None:
                    inst.then_inc(SEMO[inc], 1)

        @block.vector
        def _(vector):
            for ws, fn, inc in ops["v"]:
                for s_, v_ in ws:
                    vector.wait_ge(s_, v_)
                inst = fn(vector)
                if inc is not None and inst is not None:
                    inst.then_inc(SEMO[inc], 1)

        @block.scalar
        def _(scalar):
            for ws, fn, inc in ops["s"]:
                for s_, v_ in ws:
                    scalar.wait_ge(s_, v_)
                inst = fn(scalar)
                if inc is not None and inst is not None:
                    inst.then_inc(SEMO[inc], 1)

        @block.sync
        def _(sync):
            for ws, fn, inc in ops["y"]:
                for s_, v_ in ws:
                    sync.wait_ge(s_, v_)
                fn(sync)

        @block.gpsimd
        def _(gpsimd):
            for ws, fn, inc in ops["g"]:
                for s_, v_ in ws:
                    gpsimd.wait_ge(s_, v_)
                fn(gpsimd)

    es.close()
    return nc


def kernel(L0_attn_x, L0_attn_mix_k, L0_attn_mix_v, L0_attn_mix_r,
           L0_attn_Wk, L0_attn_Wv, L0_attn_Wr, L0_attn_Wo,
           L0_attn_time_decay, L0_attn_time_first,
           L0_ffn_W_r, L0_ffn_W_k, L0_ffn_W_v, L0_ffn_mu_r, L0_ffn_mu_k):
    f32 = np.float32
    x = np.ascontiguousarray(np.asarray(L0_attn_x, dtype=f32))

    def wt(w):
        return np.ascontiguousarray(np.asarray(w, dtype=f32).T)

    weights = {
        "fwrt": wt(L0_ffn_W_r), "fwkt": wt(L0_ffn_W_k), "fwvt": wt(L0_ffn_W_v),
        "wkt": wt(L0_attn_Wk), "wvt": wt(L0_attn_Wv), "wrt": wt(L0_attn_Wr),
        "wot": wt(L0_attn_Wo),
    }

    def mutile(v):
        return np.ascontiguousarray(
            np.asarray(v, dtype=f32).reshape(-1).reshape(8, 128).T)

    mus = {
        "murt": mutile(L0_ffn_mu_r), "mukt": mutile(L0_ffn_mu_k),
        "mxkt": mutile(L0_attn_mix_k), "mxvt": mutile(L0_attn_mix_v),
        "mxrt": mutile(L0_attn_mix_r),
    }
    td = np.asarray(L0_attn_time_decay, dtype=f32).reshape(H)
    tf = np.asarray(L0_attn_time_first, dtype=f32).reshape(H)
    tdr = np.ascontiguousarray(np.broadcast_to(td[None, :], (128, H)))
    tfr = np.ascontiguousarray(np.broadcast_to(tf[None, :], (128, H)))

    j = np.arange(128, dtype=np.float64)
    consts = {
        "tri": (lambda t: (t.__setitem__((slice(None), 0), 1.0), t)[1])(
            np.triu(np.ones((128, 128), f32), 1)),
        "idf": np.eye(128, dtype=f32),
        "ones1": np.ones((1, 128), f32),
        "cju": (CCENT - j).astype(f32).reshape(128, 1),
        "cjw": (j - 1.0 - CCENT).astype(f32).reshape(128, 1),
        "tdr": tdr, "tfr": tfr,
    }

    nc = _build_nc()
    in_maps = []
    for c in range(NCORES):
        m = {"x": np.ascontiguousarray(
            x[2 * c:2 * c + 2].reshape(ROWS, C))}
        m.update(weights)
        m.update(mus)
        m.update(consts)
        in_maps.append(m)

    res = run_bass_kernel_spmd(nc, in_maps, core_ids=list(range(NCORES)))
    out = np.stack([np.asarray(r["out"], dtype=f32) for r in res.results])
    return out.reshape(16, 2048, 1024)
